# version 09 (final)
# baseline (speedup 1.0000x reference)
from contextlib import ExitStack

import numpy as np
import ml_dtypes

import concourse.bacc as bacc
import concourse.mybir as mybir
from concourse.bass_utils import run_bass_kernel_spmd

F32 = mybir.dt.float32
BF16 = mybir.dt.bfloat16
I16 = mybir.dt.int16
U32 = mybir.dt.uint32
ALU = mybir.AluOpType
AF = mybir.ActivationFunctionType

DM = 1024
E = 2048
NH = 16
HD = 128
CHK = 256
DIN = 4240
CONVD = 2176
FF = 4096
NEXP = 8
EPS = 1e-5
DE = DM + 128


class P:
    def __init__(self, nc, stack):
        self.nc = nc
        self.stack = stack
        self.s = {}

    def sem(self, name):
        self.s[name] = [self.stack.enter_context(self.nc.semaphore(name)), 0]

    def inc(self, instr, name, k=1):
        e = self.s[name]
        e[1] += k
        return instr.then_inc(e[0], k)

    def w(self, eng, *names):
        for n in names:
            eng.wait_ge(self.s[n][0], self.s[n][1])

    def wl(self, eng, name, lag):
        e = self.s[name]
        if e[1] - lag > 0:
            eng.wait_ge(e[0], e[1] - lag)


def build(T, ncores, nsel, upto="full"):
    NCHUNK = T // CHK
    TOT = T * ncores
    KSEL = TOT // 4
    DUMMY = T

    nc = bacc.Bacc("TRN2", detect_race_conditions=False)
    big = ExitStack()
    g = P(nc, big)
    SB = lambda n, sh, dt: big.enter_context(nc.sbuf_tensor(n, sh, dt))

    DT = nc.dram_tensor
    u = DT("u", [T, DM], F32, kind="ExternalInput").ap()
    w_inT = DT("w_inT", [DM, DIN], F32, kind="ExternalInput").ap()
    conv_w = DT("conv_w", [CONVD, 4], F32, kind="ExternalInput").ap()
    conv_b = DT("conv_b", [1, CONVD], F32, kind="ExternalInput").ap()
    dt_b_c = DT("dt_b_c", [NH, 1], F32, kind="ExternalInput").ap()
    a_log_c = DT("a_log_c", [NH, 1], F32, kind="ExternalInput").ap()
    d_p_c = DT("d_p_c", [NH, 1], F32, kind="ExternalInput").ap()
    norm_w = DT("norm_w", [1, E], F32, kind="ExternalInput").ap()
    w_outT = DT("w_outT", [E, DM], F32, kind="ExternalInput").ap()
    wg = DT("wg", [DM, NEXP], F32, kind="ExternalInput").ap()
    w1 = DT("w1", [NEXP, DM, FF], F32, kind="ExternalInput").ap()
    b1 = DT("b1", [NEXP, 1, FF], F32, kind="ExternalInput").ap()
    w2 = DT("w2", [NEXP, FF, DM], F32, kind="ExternalInput").ap()
    b2 = DT("b2", [NEXP, 1, DM], F32, kind="ExternalInput").ap()
    c_row = DT("c_row", [1, DM], F32, kind="ExternalInput").ap()
    wmod = DT("wmod", [DM, 6 * DM], F32, kind="ExternalInput").ap()

    out = DT("out", [T + 1, DM], F32, kind="ExternalOutput")

    h2e = DT("h2e", [T + 1, DE], BF16, kind="Internal")
    winT_b = DT("winT_b", [128, 8, DIN], BF16, kind="Internal")
    woutT_b = DT("woutT_b", [128, 16, DM], BF16, kind="Internal")
    cc_in = DT("cc_in", [NEXP, T], F32, kind="Internal")
    cc_out = DT("cc_out", [ncores, NEXP, T], F32, kind="Internal")
    scat_dbg = DT("scat_dbg", [128, nsel // 128, DM], F32, kind="Internal")

    for s in ("s_ld", "s_st", "s_gld", "s_gst", "s_gat",
              "c_t", "c_v", "c_s", "c_g", "c_cc", "c_w"):
        g.sem(s)

    tn, vec, sc, gp, sy = nc.tensor, nc.vector, nc.scalar, nc.gpsimd, nc.sync

    def ldma(eng, dst, src, sem, **kw):
        return g.inc(eng.dma_start(dst, src, **kw), sem, 16)

    def mmacc(pm, pairs, sem="c_t"):
        n = len(pairs)
        ins = None
        for i, (l, r) in enumerate(pairs):
            ins = tn.matmul(pm, l, r, start=(i == 0), stop=(i == n - 1))
        return g.inc(ins, sem)

    ident_b = SB("ident_b", [128, 128], BF16)
    ident_f = SB("ident_f", [128, 128], F32)
    ones_f = SB("ones_f", [1, 128], F32)
    onesc_b = SB("onesc_b", [128, 1], BF16)
    trimask = SB("trimask", [128, 2, CHK], F32)
    smalls = SB("smalls", [NH, 8], F32)
    abc = SB("abc", [128, NH], F32)
    dbc = SB("dbc", [128, NH], F32)
    nwc = SB("nwc", [128, E // 128], F32)
    cwx = SB("cwx", [128, 16, 4], F32)
    cbx = SB("cbx", [128, 16], F32)
    cwB = SB("cwB", [64, 4], F32)
    cbB = SB("cbB", [64, 1], F32)
    cwC = SB("cwC", [64, 4], F32)
    cbC = SB("cbC", [64, 1], F32)
    CA1 = SB("CA1", [128, DM], F32)
    SA = SB("SA", [128, DM], F32)
    CB1 = SB("CB1", [128, DM], F32)
    SBt = SB("SBt", [128, DM], F32)
    GA = SB("GA", [128, DM], F32)
    GB = SB("GB", [128, DM], F32)
    ssd_S = SB("ssd_S", [64, E], F32)
    ssd_Sb = SB("ssd_Sb", [64, E], BF16)
    logit_sb = SB("logit_sb", [NEXP, CHK], F32)
    wgt = SB("wgt", [128, 8, NEXP], F32)
    zrow_b = SB("zrow_b", [1, DE], BF16)
    konst = SB("konst", [128, 4], F32)

    su = ExitStack()
    SU = lambda n, sh, dt: su.enter_context(nc.sbuf_tensor(n, sh, dt))
    PSU = lambda n, sh: su.enter_context(nc.psum_tensor(n, sh, F32))
    mrow = SU("mrow", [1, 6 * DM], F32)
    crow = SU("crow", [1, DM], F32)
    ccol = SU("ccol", [128, 8], F32)
    wstg = SU("wstg", [128, 2, 4480], BF16)
    t2 = SU("t2", [1, 4 * DM], F32)
    t3 = SU("t3", [1, 2 * DM], F32)
    p_u1 = PSU("p_u1", [128, 512])
    p_u2 = PSU("p_u2", [1, 512])

    vec.memset(konst[:, 0:1], EPS)
    vec.memset(konst[:, 1:2], -0.5)
    vec.memset(konst[:, 2:3], -1.0)
    vec.memset(konst[:, 3:4], 1.0)
    nc.const_aps.register(F32, EPS, konst[:, 0:1])
    nc.const_aps.register(F32, -0.5, konst[:, 1:2])
    nc.const_aps.register(F32, -1.0, konst[:, 2:3])
    nc.const_aps.register(F32, 1.0, konst[:, 3:4])
    vec.memset(ident_b[:], 1.0)
    vec.memset(ident_f[:], 1.0)
    vec.memset(ones_f[:], 1.0)
    vec.memset(onesc_b[:], 1.0)
    vec.memset(zrow_b[:], 0.0)
    g.inc(vec.memset(trimask[:], 0.0), "c_v")
    g.w(gp, "c_v")
    gp.affine_select(ident_b[:], ident_b[:], [[-1, 128]], ALU.is_equal, 0.0,
                     base=0, channel_multiplier=1)
    gp.affine_select(ident_f[:], ident_f[:], [[-1, 128]], ALU.is_equal, 0.0,
                     base=0, channel_multiplier=1)
    for kt in range(2):
        g.inc(gp.affine_select(trimask[:, kt, :], trimask[:, kt, :], [[1, CHK]],
                               ALU.is_ge, -1e30, base=-128 * kt,
                               channel_multiplier=-1), "c_g")

    ldma(sy, smalls[:, 0:1], dt_b_c, "s_ld")
    ldma(sy, smalls[:, 3:4], a_log_c, "s_ld")
    ldma(sy, smalls[:, 2:3], d_p_c, "s_ld")
    ldma(sy, nwc[:], norm_w.rearrange("o (t p) -> (o p) t", p=128), "s_ld", allow_slow_non_contiguous=True)
    ldma(sy, cwx[:], conv_w[0:2048].rearrange("(t p) w -> p t w", p=128), "s_ld")
    ldma(sy, cbx[:], conv_b[0:1, 0:2048].rearrange("o (t p) -> (o p) t", p=128), "s_ld", allow_slow_non_contiguous=True)
    ldma(sy, cwB[:], conv_w[2048:2112], "s_ld")
    ldma(sy, cbB[:], conv_b[0:1, 2048:2112].rearrange("o (a p) -> (o p) a", p=64), "s_ld")
    ldma(sy, cwC[:], conv_w[2112:2176], "s_ld")
    ldma(sy, cbC[:], conv_b[0:1, 2112:2176].rearrange("o (a p) -> (o p) a", p=64), "s_ld")
    ldma(sy, crow[:], c_row, "s_ld")
    ldma(sy, wgt[:], wg.rearrange("(c p) e -> p c e", p=128), "s_ld")
    g.w(gp, "c_v")
    g.inc(gp.dma_start(h2e.ap()[T:T + 1, :], zrow_b[:]), "s_gst", 16)

    g.w(sc, "s_ld")
    g.inc(sc.activation(smalls[:, 4:5], smalls[:, 3:4], AF.Exp), "c_s")
    g.w(vec, "c_s")
    g.inc(vec.tensor_scalar(smalls[:, 1:2], smalls[:, 4:5], -1.0, None, ALU.mult), "c_v")
    g.w(tn, "c_v")
    tn.transpose(p_u1[0:1, 0:NH], smalls[:, 1:2], ident_f[0:NH, 0:NH])
    g.inc(tn.transpose(p_u1[0:1, 64:64 + NH], smalls[:, 2:3], ident_f[0:NH, 0:NH]), "c_t")
    g.w(vec, "c_t")
    vec.tensor_copy(t3[0:1, 0:NH], p_u1[0:1, 0:NH])
    g.inc(vec.tensor_copy(t3[0:1, 64:64 + NH], p_u1[0:1, 64:64 + NH]), "c_v")
    g.w(tn, "c_v")
    tn.matmul(p_u1[:, 0:NH], ones_f[:], t3[0:1, 0:NH], start=True, stop=True)
    g.inc(tn.matmul(p_u1[:, 64:64 + NH], ones_f[:], t3[0:1, 64:64 + NH],
                    start=True, stop=True), "c_t")
    g.w(vec, "c_t")
    vec.tensor_copy(abc[:], p_u1[:, 0:NH])
    g.inc(vec.tensor_copy(dbc[:], p_u1[:, 64:64 + NH]), "c_v")

    g.inc(sc.activation(crow[:], crow[:], AF.Silu), "c_s")
    g.w(tn, "c_s")
    ins = None
    for dc in range(8):
        ins = tn.transpose(p_u1[:, 32 + dc:32 + dc + 1], crow[0:1, dc * 128:(dc + 1) * 128],
                           ident_f[0:1, 0:1])
    g.inc(ins, "c_t")
    g.w(vec, "c_t")
    g.inc(vec.tensor_copy(ccol[:], p_u1[:, 32:40]), "c_v")
    wmv = wmod.rearrange("(c p) n -> p c n", p=128)
    wstg_f = wstg.bitcast(F32)
    for nx in range(24):
        buf = wstg_f[:, nx % 2, 0:2048].rearrange("p (c n) -> p c n", c=8)
        if nx >= 2:
            g.w(gp, "c_t")
        ldma(gp, buf, wmv[:, :, nx * 256:(nx + 1) * 256], "s_gld")
        g.w(tn, "s_gld", "c_v")
        mmacc(p_u2[0:1, 0:256], [(ccol[:, dc:dc + 1], buf[:, dc, :]) for dc in range(8)])
        g.w(vec, "c_t")
        g.inc(vec.tensor_copy(mrow[0:1, nx * 256:(nx + 1) * 256], p_u2[0:1, 0:256]), "c_v")
    g.w(sc, "c_v")
    sc.activation(t2[0:1, 0:DM], mrow[0:1, 2 * DM:3 * DM], AF.Abs)
    g.inc(sc.activation(t2[0:1, DM:2 * DM], mrow[0:1, 5 * DM:6 * DM], AF.Abs), "c_s")
    g.w(vec, "c_s")
    vec.tensor_scalar(t3[0:1, 0:DM], mrow[0:1, 2 * DM:3 * DM], 0.0, None, ALU.is_ge)
    g.inc(vec.tensor_scalar(t3[0:1, DM:2 * DM], mrow[0:1, 5 * DM:6 * DM], 0.0, None,
                            ALU.is_ge), "c_v")
    vec.tensor_scalar(t2[0:1, 0:2 * DM], t2[0:1, 0:2 * DM], 0.25, None, ALU.max)
    g.inc(vec.tensor_scalar(t3[0:1, 0:2 * DM], t3[0:1, 0:2 * DM], 2.0, -1.0,
                            ALU.mult, ALU.add), "c_v")
    g.w(vec, "c_v")
    g.inc(vec.tensor_tensor(t2[0:1, 0:2 * DM], t2[0:1, 0:2 * DM], t3[0:1, 0:2 * DM],
                            ALU.mult), "c_v")
    vec.tensor_scalar(t2[0:1, 2 * DM:3 * DM], mrow[0:1, 1 * DM:2 * DM], 1.0, None, ALU.add)
    g.inc(vec.tensor_scalar(t2[0:1, 3 * DM:4 * DM], mrow[0:1, 4 * DM:5 * DM], 1.0, None,
                            ALU.add), "c_v")
    g.w(tn, "c_v")
    for dst, src in ((CA1, t2[0:1, 2 * DM:3 * DM]), (SA, mrow[0:1, 0:DM]),
                     (CB1, t2[0:1, 3 * DM:4 * DM]), (SBt, mrow[0:1, 3 * DM:4 * DM]),
                     (GA, t2[0:1, 0:DM]), (GB, t2[0:1, DM:2 * DM])):
        for hh in range(2):
            g.w(tn, "c_v")
            g.inc(tn.matmul(p_u1[:, 0:512], ones_f[:],
                            src[0:1, hh * 512:(hh + 1) * 512], start=True, stop=True), "c_t")
            g.w(vec, "c_t")
            g.inc(vec.tensor_copy(dst[:, hh * 512:(hh + 1) * 512], p_u1[:, 0:512]), "c_v")

    winTv = w_inT.rearrange("(c p) f -> p c f", p=128)
    g.w(gp, "c_t")
    for i in range(8):
        f0 = i * 530
        buf = wstg[:, i % 2, 0:4240].rearrange("p (c f) -> p c f", c=8)
        if i >= 2:
            g.w(gp, "s_gst")
        ldma(gp, buf, winTv[:, :, f0:f0 + 530], "s_gld")
        g.w(gp, "s_gld")
        g.inc(gp.dma_start(winT_b.ap()[:, :, f0:f0 + 530], buf), "s_gst", 16)
    woutTv = w_outT.rearrange("(c p) f -> p c f", p=128)
    for i in range(4):
        buf = wstg[:, i % 2, 0:4096].rearrange("p (c f) -> p c f", c=4)
        g.w(gp, "s_gst")
        ldma(gp, buf, woutTv[:, i * 4:(i + 1) * 4, :], "s_gld")
        g.w(gp, "s_gld")
        g.inc(gp.dma_start(woutT_b.ap()[:, i * 4:(i + 1) * 4, :], buf), "s_gst", 16)
    vec.memset(ssd_S[:], 0.0)
    g.inc(vec.memset(ssd_Sb[:], 0.0), "c_v")
    g.w(gp, "c_v")
    g.w(sy, "s_gst")
    su.close()

    mp = ExitStack()
    M = lambda n, sh, dt: mp.enter_context(nc.sbuf_tensor(n, sh, dt))
    PM = lambda n, sh: mp.enter_context(nc.psum_tensor(n, sh, F32))
    x_tm = M("x_tm", [128, 2, DM], F32)
    stat = M("stat", [128, 2, 8], F32)
    h_f32 = M("h_f32", [128, 2, DM], F32)
    h_tm = M("h_tm", [128, 2, DM], BF16)
    h_fm = M("h_fm", [128, 8, CHK], BF16)
    winq = M("winq", [128, 2, 8, 640], BF16)
    woth = M("woth", [128, 8, DM], BF16)
    z_b = M("z_b", [128, 16, CHK], BF16)
    xraw = M("xraw", [128, 16, CHK + 3], F32)
    Braw = M("Braw", [64, CHK + 3], F32)
    Craw = M("Craw", [64, CHK + 3], F32)
    xpost = M("xpost", [128, 16, CHK], BF16)
    Bpost = M("Bpost", [64, CHK], BF16)
    Cpost = M("Cpost", [64, CHK], BF16)
    Cpf = M("Cpf", [64, CHK], F32)
    dt_sb = M("dt_sb", [NH, CHK], F32)
    a_rm = M("a_rm", [NH, CHK], F32)
    acum_rm = M("acum_rm", [NH, CHK], F32)
    acfl = M("acfl", [1, NH * CHK], F32)
    ce_f = M("ce_f", [64, 2, CHK], F32)
    aclrow = M("aclrow", [1, NH], F32)
    cdrow = M("cdrow", [1, NH], F32)
    dtc_tm = M("dtc_tm", [128, 2, NH], F32)
    acum_tm = M("acum_tm", [128, 2, NH], F32)
    w_tm = M("w_tm", [128, 2, NH], F32)
    cdcol = M("cdcol", [64, NH], F32)
    xcT = M("xcT", [128, 2, E], BF16)
    wx_b = M("wx_b", [128, 2, HD], BF16)
    Bt = M("Bt", [128, 2, 64], BF16)
    scot = M("scot", [128, 2, CHK], F32)
    dcy = M("dcy", [128, 2, CHK], F32)
    mt_b = M("mt_b", [128, 2, CHK], BF16)
    ce_b = M("ce_b", [64, 2, CHK], BF16)
    yg_s = M("yg_s", [128, 2, CHK], F32)
    ysq_b = M("ysq_b", [128, 2, CHK], BF16)
    rms_r = M("rms_r", [1, CHK], F32)
    rbc = M("rbc", [128, CHK], F32)
    y_b = M("y_b", [128, 16, CHK], BF16)
    x2_tm = M("x2_tm", [128, 2, DM], F32)
    
    h2fm = M("h2fm", [128, 8, 128], F32)
    h2es = M("h2es", [128, 2, DE], BF16)
    p_a = PM("p_a", [128, 512])
    p_b = PM("p_b", [128, 512])
    p_c = PM("p_c", [128, 512])
    p_d = PM("p_d", [128, 512])
    p_st = PM("p_st", [64, 1024])
    p_sm = PM("p_sm", [128, 512])
    p_lg = PM("p_lg", [8, 512])

    vec.memset(h2es[:], 0.0)
    vec.memset(xraw[:], 0.0)
    vec.memset(Braw[:], 0.0)
    g.inc(vec.memset(Craw[:], 0.0), "c_v")

    WGRP = [(0, 5), (5, 9), (9, 13), (13, 17), (17, 21), (21, 25), (25, 30), (30, 34)]
    uv = u.rearrange("(a p) d -> p a d", p=128)
    ov = out.ap()[0:T].rearrange("(a p) d -> p a d", p=128)
    h2ev = h2e.ap()[0:T].rearrange("(a p) d -> p a d", p=128)

    def ln_block(src2, dst2, Ctile, Stile, out_bf=None):
        g.w(sc, "c_v")
        for a in range(2):
            sc.activation(dst2[:, a, :], src2[:, a, :], AF.Copy, accum_out=stat[:, a, 0:1])
            g.inc(sc.activation(dst2[:, a, :], src2[:, a, :], AF.Square,
                                accum_out=stat[:, a, 1:2]), "c_s")
        g.w(vec, "c_s")
        for a in range(2):
            vec.tensor_scalar(stat[:, a, 2:3], stat[:, a, 0:1], 1.0 / DM, None, ALU.mult)
            vec.tensor_scalar(stat[:, a, 3:4], stat[:, a, 1:2], 1.0 / DM, None, ALU.mult)
        for a in range(2):
            vec.tensor_tensor(stat[:, a, 4:5], stat[:, a, 2:3], stat[:, a, 2:3], ALU.mult)
        for a in range(2):
            g.inc(vec.tensor_tensor(stat[:, a, 5:6], stat[:, a, 3:4], stat[:, a, 4:5],
                                    ALU.subtract), "c_v")
        g.w(sc, "c_v")
        for a in range(2):
            sc.activation(stat[:, a, 6:7], stat[:, a, 5:6], AF.Ln, bias=EPS)
        for a in range(2):
            g.inc(sc.activation(stat[:, a, 7:8], stat[:, a, 6:7], AF.Exp, scale=-0.5), "c_s")
        g.w(vec, "c_s")
        for a in range(2):
            vec.tensor_scalar(dst2[:, a, :], src2[:, a, :], stat[:, a, 2:3],
                              stat[:, a, 7:8], ALU.subtract, ALU.mult)
        for a in range(2):
            vec.tensor_tensor(dst2[:, a, :], dst2[:, a, :], Ctile[:], ALU.mult)
        for a in range(2):
            tgt = dst2[:, a, :] if out_bf is None else out_bf[:, a, :]
            g.inc(vec.tensor_tensor(tgt, dst2[:, a, :], Stile[:], ALU.add), "c_v")

    for ci in range(NCHUNK):
        par = ci % 2
        t0 = ci * CHK
        g.w(sy, "c_v")
        ldma(sy, x_tm[:], uv[:, 2 * ci:2 * ci + 2, :], "s_ld")
        g.w(sc, "s_ld")
        ln_block(x_tm, h_f32, CA1, SA, out_bf=h_tm)
        g.w(tn, "c_v")
        for a in range(2):
            for dc in range(8):
                pt = (p_c if (a * 8 + dc) % 2 == 0 else p_d).bitcast(BF16)
                g.wl(tn, "c_v", 1)
                g.inc(tn.transpose(pt[:, 0:128], h_tm[:, a, dc * 128:(dc + 1) * 128],
                                   ident_b[:]), "c_t")
                g.w(vec, "c_t")
                g.inc(vec.tensor_copy(h_fm[:, dc, a * 128:(a + 1) * 128], pt[:, 0:128]), "c_v")
        for g8, (ta, tb) in enumerate(WGRP):
            wb = winq[:, g8 % 2]
            f0 = ta * 128
            fn = min(tb * 128, DIN) - f0
            g.wl(gp, "c_w", 1)
            ldma(gp, wb[:, :, 0:fn], winT_b.ap()[:, :, f0:f0 + fn], "s_gld")
            g.w(tn, "s_gld")
            for t in range(ta, tb):
                off = t * 128 - f0
                g.w(tn, "c_v", "c_s")
                if t == 32:
                    for half in range(2):
                        pm = p_a if half == 0 else p_b
                        g.w(tn, "c_v")
                        mmacc(pm[0:64, 0:CHK],
                              [(wb[:, dc, off + 64 * half:off + 64 * half + 64],
                                h_fm[:, dc, :]) for dc in range(8)])
                        g.w(vec, "c_t")
                        raw = Braw if half == 0 else Craw
                        g.inc(vec.tensor_copy(raw[:, 3:3 + CHK], pm[0:64, 0:CHK]), "c_v")
                    continue
                prow = 128 if t < 33 else 16
                pm = p_a if t % 2 == 0 else p_b
                mmacc(pm[0:prow, 0:CHK],
                      [(wb[:, dc, off:off + prow], h_fm[:, dc, :]) for dc in range(8)])
                if t < 16:
                    g.w(sc, "c_t")
                    g.inc(sc.activation(z_b[:, t, :], pm[:, 0:CHK], AF.Silu), "c_s")
                elif t < 32:
                    g.w(vec, "c_t")
                    g.inc(vec.tensor_copy(xraw[:, t - 16, 3:3 + CHK], pm[:, 0:CHK]), "c_v")
                else:
                    g.w(vec, "c_t")
                    g.inc(vec.tensor_scalar(dt_sb[:], pm[0:NH, 0:CHK], smalls[:, 0:1],
                                            None, ALU.add), "c_v")
                    g.w(sc, "c_v")
                    g.inc(sc.activation(a_rm[:], dt_sb[:], AF.Abs), "c_s")
                    g.w(sc, "c_s")
                    g.inc(sc.activation(a_rm[:], a_rm[:], AF.Exp, scale=-1.0), "c_s")
                    g.w(sc, "c_s")
                    g.inc(sc.activation(a_rm[:], a_rm[:], AF.Ln, bias=1.0), "c_s")
                    g.w(sc, "c_s")
                    g.inc(sc.activation(acum_rm[:], dt_sb[:], AF.Relu), "c_s")
                    g.w(vec, "c_s")
                    g.inc(vec.tensor_tensor(dt_sb[:], acum_rm[:], a_rm[:], ALU.add), "c_v")
            g.inc(tn.nop(), "c_w")
        g.w(vec, "c_v", "c_s")
        for t in range(16):
            pm = p_c if t % 2 == 0 else p_d
            if t >= 2:
                g.w(vec, "c_s")
            vec.tensor_scalar(pm[:, 0:CHK], xraw[:, t, 0:CHK], cwx[:, t, 0:1],
                              cbx[:, t:t + 1], ALU.mult, ALU.add)
            for wv in range(1, 4):
                vec.scalar_tensor_tensor(pm[:, 0:CHK], xraw[:, t, wv:wv + CHK],
                                         cwx[:, t, wv:wv + 1], pm[:, 0:CHK],
                                         ALU.mult, ALU.add)
            g.inc(vec.tensor_copy(xraw[:, t, 0:3], xraw[:, t, CHK:CHK + 3]), "c_v")
            g.w(sc, "c_v")
            g.inc(sc.activation(xpost[:, t, :], pm[:, 0:CHK], AF.Silu), "c_s")
        for bc in range(2):
            raw, po, cw, cb = ((Braw, Bpost, cwB, cbB), (Craw, Cpost, cwC, cbC))[bc]
            pm = p_c if bc == 0 else p_d
            g.w(vec, "c_s")
            vec.tensor_scalar(pm[0:64, 0:CHK], raw[:, 0:CHK], cw[:, 0:1], cb[:],
                              ALU.mult, ALU.add)
            for wv in range(1, 4):
                vec.scalar_tensor_tensor(pm[0:64, 0:CHK], raw[:, wv:wv + CHK],
                                         cw[:, wv:wv + 1], pm[0:64, 0:CHK],
                                         ALU.mult, ALU.add)
            g.inc(vec.tensor_copy(raw[:, 0:3], raw[:, CHK:CHK + 3]), "c_v")
            g.w(sc, "c_v")
            g.inc(sc.activation(po[:], pm[0:64, 0:CHK], AF.Silu), "c_s")
            if bc == 1:
                g.inc(sc.activation(Cpf[:], pm[0:64, 0:CHK], AF.Silu), "c_s")
        g.w(vec, "c_s")
        g.inc(vec.tensor_scalar(a_rm[:], dt_sb[:], smalls[:, 1:2], None, ALU.mult), "c_v")
        g.w(vec, "c_v")
        g.inc(vec.tensor_tensor_scan(acum_rm[:], a_rm[:], a_rm[:], 0.0, ALU.add,
                                     ALU.bypass), "c_v")
        g.w(gp, "c_v")
        g.inc(gp.dma_start(acfl[:], acum_rm[:]), "s_gst", 16)
        g.w(tn, "c_v")
        for kt in range(2):
            if kt:
                g.w(tn, "c_v")
            tn.transpose(p_c[:, 0:NH], dt_sb[:, kt * 128:(kt + 1) * 128],
                         ident_f[0:NH, 0:NH])
            g.inc(tn.transpose(p_d[:, 0:NH], acum_rm[:, kt * 128:(kt + 1) * 128],
                               ident_f[0:NH, 0:NH]), "c_t")
            g.w(vec, "c_t")
            vec.tensor_copy(dtc_tm[:, kt, :], p_c[:, 0:NH])
            g.inc(vec.tensor_copy(acum_tm[:, kt, :], p_d[:, 0:NH]), "c_v")
        g.w(tn, "c_v")
        g.inc(tn.transpose(p_sm[0:1, 0:NH], acum_rm[:, CHK - 1:CHK],
                           ident_f[0:NH, 0:NH]), "c_t")
        g.w(vec, "c_t")
        g.inc(vec.tensor_copy(aclrow[:], p_sm[0:1, 0:NH]), "c_v")
        g.w(sc, "c_v")
        g.inc(sc.activation(cdrow[:], aclrow[:], AF.Exp), "c_s")
        g.w(tn, "c_v", "c_s")
        tn.matmul(p_sm[:, 0:NH], ones_f[:], aclrow[:], start=True, stop=True)
        g.inc(tn.matmul(p_sm[0:64, 64:64 + NH], ones_f[0:1, 0:64], cdrow[:],
                        start=True, stop=True), "c_t")
        g.w(vec, "c_t")
        g.inc(vec.tensor_copy(cdcol[:], p_sm[0:64, 64:64 + NH]), "c_v")
        for kt in range(2):
            vec.tensor_tensor(w_tm[:, kt, :], p_sm[:, 0:NH], acum_tm[:, kt, :],
                              ALU.subtract)
        g.inc(vec.engine_nop(), "c_v")
        g.w(sc, "c_v")
        g.inc(sc.activation(w_tm[:], w_tm[:], AF.Exp), "c_s")
        g.w(vec, "c_s")
        g.inc(vec.tensor_tensor(w_tm[:], w_tm[:], dtc_tm[:], ALU.mult), "c_v")
        g.w(tn, "c_v", "c_s")
        for kt in range(2):
            for t in range(16):
                pt = (p_c if t % 2 == 0 else p_d).bitcast(BF16)
                if kt == 1 and t == 0:
                    g.w(tn, "c_v")
                else:
                    g.wl(tn, "c_v", 1)
                g.inc(tn.transpose(pt[:, 0:128], xpost[:, t, kt * 128:(kt + 1) * 128],
                                   ident_b[:]), "c_t")
                g.w(vec, "c_t")
                g.inc(vec.tensor_copy(xcT[:, kt, t * 128:(t + 1) * 128], pt[:, 0:128]), "c_v")
            g.w(tn, "c_v")
            g.inc(tn.transpose(p_c.bitcast(BF16)[:, 0:64],
                               Bpost[:, kt * 128:(kt + 1) * 128], ident_b[0:64, 0:64]), "c_t")
            g.w(vec, "c_t")
            g.inc(vec.tensor_copy(Bt[:, kt, :], p_c.bitcast(BF16)[:, 0:64]), "c_v")
        g.w(tn, "c_v")
        for kt in range(2):
            pm = p_a if kt == 0 else p_b
            g.inc(tn.matmul(pm[:, 0:CHK], Bpost[:, kt * 128:(kt + 1) * 128], Cpost[:],
                            start=True, stop=True), "c_t")
            g.w(vec, "c_t")
            g.inc(vec.tensor_copy(scot[:, kt, :], pm[:, 0:CHK]), "c_v")
        for h in range(NH):
            hp = h % 2
            pmy = p_a if hp == 0 else p_b
            g.w(tn, "c_v", "c_s", "s_gst")
            ins = None
            for kt in range(2):
                ins = tn.matmul(p_sm[:, kt * CHK:(kt + 1) * CHK], ones_f[:],
                                acfl[0:1, h * CHK:(h + 1) * CHK], start=True, stop=True)
            g.inc(ins, "c_t")
            g.w(vec, "c_t")
            ins = None
            for kt in range(2):
                ins = vec.scalar_tensor_tensor(dcy[:, kt, :], p_sm[:, kt * CHK:(kt + 1) * CHK],
                                               acum_tm[:, kt, h:h + 1], trimask[:, kt, :],
                                               ALU.subtract, ALU.add)
            g.inc(ins, "c_v")
            g.w(sc, "c_v")
            sc.activation(ce_f[:, hp, :], p_sm[0:64, 0:CHK], AF.Exp)
            g.inc(sc.activation(dcy[:], dcy[:], AF.Exp), "c_s")
            g.w(vec, "c_s")
            g.inc(vec.tensor_tensor(ce_b[:, hp, :], ce_f[:, hp, :], Cpf[:], ALU.mult), "c_v")
            g.w(vec, "c_s")
            for kt in range(2):
                vec.tensor_tensor(dcy[:, kt, :], scot[:, kt, :], dcy[:, kt, :], ALU.mult)
            g.inc(vec.engine_nop(), "c_v")
            g.w(vec, "c_v")
            for kt in range(2):
                vec.tensor_scalar(mt_b[:, kt, :], dcy[:, kt, :], dtc_tm[:, kt, h:h + 1],
                                  None, ALU.mult)
            for kt in range(2):
                g.inc(vec.tensor_scalar(wx_b[:, kt, :], xcT[:, kt, h * HD:(h + 1) * HD],
                                        w_tm[:, kt, h:h + 1], None, ALU.mult), "c_v")
            g.w(tn, "c_v")
            for kt in range(2):
                tn.matmul(pmy[:, 0:CHK], xcT[:, kt, h * HD:(h + 1) * HD], mt_b[:, kt, :],
                          start=(kt == 0), stop=False)
            g.inc(tn.matmul(pmy[:, 0:CHK], ssd_Sb[:, h * HD:(h + 1) * HD], ce_b[:, hp, :],
                            start=False, stop=True), "c_t")
            ins = None
            for kt in range(2):
                ins = tn.matmul(p_st[:, (h % 8) * HD:(h % 8 + 1) * HD], Bt[:, kt, :],
                                wx_b[:, kt, :], start=(kt == 0), stop=(kt == 1))
            g.inc(ins, "c_t")
            g.w(vec, "c_t")
            vec.scalar_tensor_tensor(yg_s[:, hp, :], xpost[:, h, :], dbc[:, h:h + 1],
                                     pmy[:, 0:CHK], ALU.mult, ALU.add)
            g.inc(vec.engine_nop(), "c_v")
            g.w(vec, "c_v")
            g.inc(vec.tensor_tensor(yg_s[:, hp, :], yg_s[:, hp, :], z_b[:, h, :],
                                    ALU.mult), "c_v")
            g.w(vec, "c_v")
            vec.tensor_copy(y_b[:, h, :], yg_s[:, hp, :])
            g.inc(vec.tensor_tensor(ysq_b[:, hp, :], yg_s[:, hp, :], yg_s[:, hp, :],
                                    ALU.mult), "c_v")
            g.w(tn, "c_v")
            g.inc(tn.matmul(p_lg[0:1, 0:CHK], onesc_b[:], ysq_b[:, hp, :],
                            start=(h == 0), stop=(h == NH - 1)), "c_t")
            g.w(vec, "c_t")
            vec.tensor_scalar(ssd_S[:, h * HD:(h + 1) * HD], ssd_S[:, h * HD:(h + 1) * HD],
                              cdcol[:, h:h + 1], None, ALU.mult)
            g.inc(vec.engine_nop(), "c_v")
            g.w(vec, "c_v")
            g.inc(vec.tensor_tensor(ssd_S[:, h * HD:(h + 1) * HD],
                                    ssd_S[:, h * HD:(h + 1) * HD],
                                    p_st[:, (h % 8) * HD:(h % 8 + 1) * HD], ALU.add), "c_v")
        g.w(vec, "c_v")
        g.inc(vec.tensor_copy(ssd_Sb[:], ssd_S[:]), "c_v")
        g.w(vec, "c_t")
        g.inc(vec.tensor_scalar(rms_r[:], p_lg[0:1, 0:CHK], 1.0 / E, EPS,
                                ALU.mult, ALU.add), "c_v")
        g.w(sc, "c_v")
        sc.activation(rms_r[:], rms_r[:], AF.Ln)
        g.inc(sc.activation(rms_r[:], rms_r[:], AF.Exp, scale=-0.5), "c_s")
        g.w(tn, "c_s")
        g.inc(tn.matmul(p_sm[:, 0:CHK], ones_f[:], rms_r[:], start=True, stop=True), "c_t")
        g.w(vec, "c_t")
        g.inc(vec.tensor_copy(rbc[:], p_sm[:, 0:CHK]), "c_v")
        g.w(vec, "c_v")
        for h in range(NH):
            vec.tensor_tensor(y_b[:, h, :], y_b[:, h, :], rbc[:], ALU.mult)
        g.inc(vec.engine_nop(), "c_v")
        g.w(vec, "c_v")
        for h in range(NH):
            vec.tensor_scalar(y_b[:, h, :], y_b[:, h, :], nwc[:, h:h + 1], None, ALU.mult)
        g.inc(vec.engine_nop(), "c_v")
        ins = None
        for half in range(2):
            wob = woth
            g.w(gp, "c_t")
            ldma(gp, wob[:], woutT_b.ap()[:, half * 8:(half + 1) * 8, :], "s_gld")
            g.w(tn, "s_gld", "c_v")
            for tt_ in range(2):
                pmA = p_a if tt_ == 0 else p_c
                pmB = p_b if tt_ == 0 else p_d
                for ec in range(8):
                    eci = half * 8 + ec
                    tn.matmul(pmA[:, 0:512], y_b[:, eci, tt_ * 128:(tt_ + 1) * 128],
                              wob[:, ec, 0:512], start=(eci == 0), stop=(eci == 15))
                    ins = tn.matmul(pmB[:, 0:512], y_b[:, eci, tt_ * 128:(tt_ + 1) * 128],
                                    wob[:, ec, 512:1024], start=(eci == 0), stop=(eci == 15))
            g.inc(ins, "c_t")
        g.w(vec, "c_t", "s_st")
        for tt_ in range(2):
            pmA = p_a if tt_ == 0 else p_c
            pmB = p_b if tt_ == 0 else p_d
            vec.tensor_tensor(x2_tm[:, tt_, 0:512], pmA[:, 0:512], GA[:, 0:512], ALU.mult)
            vec.tensor_tensor(x2_tm[:, tt_, 512:1024], pmB[:, 0:512], GA[:, 512:1024],
                              ALU.mult)
        g.inc(vec.engine_nop(), "c_v")
        g.w(vec, "c_v")
        for tt_ in range(2):
            g.inc(vec.tensor_tensor(x2_tm[:, tt_, :], x2_tm[:, tt_, :],
                                    x_tm[:, tt_, :], ALU.add), "c_v")
        ln_block(x2_tm, h_f32, CB1, SBt)
        g.w(vec, "c_v")
        for a in range(2):
            g.inc(vec.tensor_copy(h2es[:, a, 0:DM], h_f32[:, a, :]), "c_v")
        for a in range(2):
            g.w(tn, "c_v")
            for dc in range(8):
                pt = p_c if dc % 2 == 0 else p_d
                g.wl(tn, "c_v", 1)
                g.inc(tn.transpose(pt[:, 0:128], h_f32[:, a, dc * 128:(dc + 1) * 128],
                                   ident_f[:]), "c_t")
                g.w(vec, "c_t")
                g.inc(vec.tensor_copy(h2fm[:, dc, :], pt[:, 0:128]), "c_v")
            g.w(tn, "c_v")
            mmacc(p_lg[:, a * 128:(a + 1) * 128],
                  [(wgt[:, dc, :], h2fm[:, dc, :]) for dc in range(8)])
            g.w(vec, "c_t", "s_gst")
            g.inc(vec.tensor_copy(logit_sb[:, a * 128:(a + 1) * 128],
                                  p_lg[:, a * 128:(a + 1) * 128]), "c_v")
        g.w(tn, "c_v")
        for a in range(2):
            pt = p_c if a == 0 else p_d
            g.inc(tn.transpose(pt[:, 0:NEXP],
                               logit_sb[0:NEXP, a * 128:(a + 1) * 128],
                               ident_f[0:NEXP, 0:NEXP]), "c_t")
            g.w(sc, "c_t", "s_st")
            g.inc(sc.activation(h2es[:, a, DM:DM + NEXP], pt[:, 0:NEXP], AF.Sigmoid), "c_s")
        g.w(sy, "c_v", "c_s")
        ldma(sy, ov[:, 2 * ci:2 * ci + 2, :], x2_tm[:], "s_st")
        ldma(sy, h2ev[:, 2 * ci:2 * ci + 2, :], h2es[:], "s_st")
        g.w(gp, "c_v")
        g.inc(gp.dma_start(cc_in.ap()[:, t0:t0 + CHK], logit_sb[:]), "s_gst", 16)

    mp.close()

    if upto == "x2":
        g.w(sy, "s_st")
        g.w(gp, "s_gst")
        nc.finalize()
        return nc

    fp = ExitStack()
    FM = lambda n, sh, dt: fp.enter_context(nc.sbuf_tensor(n, sh, dt))
    PF = lambda n, sh: fp.enter_context(nc.psum_tensor(n, sh, F32))
    Vv = FM("Vv", [128, TOT // 16], F32)
    m128x8 = FM("m128x8", [128, NEXP], F32)
    m8x128 = FM("m8x128", [NEXP, 128], F32)
    bst = FM("bst", [NEXP, 8], F32)
    nmid_sb = FM("nmid_sb", [128, 1], F32)
    cnt = FM("cnt", [128, 1], F32)
    vmxc = FM("vmxc", [128, 2], F32)
    mmrow = FM("mmrow", [1, 2, 128], F32)
    tvx = FM("tvx", [1, 2, NEXP], F32)
    throw = FM("throw", [1, NEXP], F32)
    onescol_f = FM("onescol_f", [128, 1], F32)
    thrb = FM("thrb", [16, NEXP], F32)
    iot2 = FM("iot2", [16, T // 16], F32)
    slot16 = FM("slot16", [16, nsel // 16], F32)
    cand = FM("cand", [16, T // 16], F32)
    mask = FM("mask", [16, T // 16], F32)
    candi = FM("candi", [16, NEXP, T // 16], I16)
    sgo = FM("sgo", [16, T // 16], I16)
    nf = FM("nf", [1, NEXP], U32)
    nff = FM("nff", [1, 1], F32)
    nfb = FM("nfb", [16, 1], F32)
    vmask = FM("vmask", [16, nsel // 16], F32)
    idxf = FM("idxf", [16, nsel // 16], F32)
    idxb = FM("idxb", [128, NEXP, nsel // 16], I16)
    gat_ch = FM("gat_ch", [128, 2, DE], BF16)
    gatF = FM("gatF", [128, 8, nsel], BF16)
    gcol = FM("gcol", [128, nsel // 128], F32)
    w1q = FM("w1q", [128, 2, 8, FF // 8], BF16)
    w2q = FM("w2q", [128, 2, 4, DM], BF16)
    b1c = FM("b1c", [128, FF // 128], F32)
    hidb = FM("hidb", [128, 2, 4, nsel], BF16)
    outacc = FM("outacc", [128, nsel // 128, DM], F32)
    b2g = FM("b2g", [128, DM], F32)
    b2row = FM("b2row", [1, DM], F32)
    gtmp8 = FM("gtmp8", [128, NEXP], BF16)
    p_h1 = PF("p_h1", [128, nsel])
    p_h2 = PF("p_h2", [128, nsel])
    p_o = PF("p_o", [128, 1024])

    g.w(gp, "s_gst", "s_st")
    g.w(vec, "s_st")
    g.w(sy, "s_st")
    g.inc(gp.collective_compute("AllGather", ALU.bypass, ins=[cc_in.ap()[:]],
                                outs=[cc_out.ap()[:]],
                                replica_groups=[list(range(ncores))]), "c_cc")
    gp.iota(iot2[:], [[16, T // 16]], base=0, channel_multiplier=1,
            allow_small_or_imprecise_dtypes=True)
    g.inc(gp.iota(slot16[:], [[16, nsel // 16]], base=0, channel_multiplier=1,
                  allow_small_or_imprecise_dtypes=True), "c_g")
    g.inc(vec.memset(idxb[:], 0), "c_v")

    FV = TOT // 16
    vec.memset(m128x8[:], 1.0)
    g.inc(vec.memset(m8x128[:], 1.0), "c_v")
    g.w(gp, "c_v")
    gp.affine_select(m128x8[:], m128x8[:], [[-16, NEXP]], ALU.is_ge, 0.0,
                     base=0, channel_multiplier=1)
    gp.affine_select(m128x8[:], m128x8[:], [[16, NEXP]], ALU.is_ge, 0.0,
                     base=15, channel_multiplier=-1)
    gp.affine_select(m8x128[:], m8x128[:], [[1, 128]], ALU.is_ge, 0.0,
                     base=0, channel_multiplier=-16)
    g.inc(gp.affine_select(m8x128[:], m8x128[:], [[-1, 128]], ALU.is_ge, 0.0,
                           base=15, channel_multiplier=16), "c_g")
    g.w(sy, "c_cc")
    PPC = 16 // ncores
    for e in range(NEXP):
        for cc_ in range(ncores):
            ldma(sy, Vv[16 * e + cc_ * PPC:16 * e + (cc_ + 1) * PPC, :],
                 cc_out.ap()[cc_, e, :].rearrange("(p f) -> p f", p=PPC), "s_ld")
    fmx = outacc[:, 0, :]
    fmn = outacc[:, 1, :]
    g.w(vec, "s_ld")
    w2_ = FV // 2
    vec.tensor_tensor(fmx[:, 0:w2_], Vv[:, 0:w2_], Vv[:, w2_:FV], ALU.max)
    g.inc(vec.tensor_tensor(fmn[:, 0:w2_], Vv[:, 0:w2_], Vv[:, w2_:FV], ALU.min),
          "c_v")
    w2_ //= 2
    while w2_ >= 1:
        g.w(vec, "c_v")
        vec.tensor_tensor(fmx[:, 0:w2_], fmx[:, 0:w2_], fmx[:, w2_:2 * w2_], ALU.max)
        g.inc(vec.tensor_tensor(fmn[:, 0:w2_], fmn[:, 0:w2_], fmn[:, w2_:2 * w2_],
                                ALU.min), "c_v")
        w2_ //= 2
    g.w(vec, "c_v")
    vec.tensor_copy(vmxc[:, 0:1], fmx[:, 0:1])
    g.inc(vec.tensor_copy(vmxc[:, 1:2], fmn[:, 0:1]), "c_v")
    g.w(tn, "c_v")
    tn.transpose(p_h2[0:1, 0:128], vmxc[:, 0:1], ident_f[:])
    g.inc(tn.transpose(p_h2[0:1, 128:256], vmxc[:, 1:2], ident_f[:]), "c_t")
    g.w(vec, "c_t")
    g.inc(vec.tensor_copy(mmrow[:], p_h2[0:1, 0:256]), "c_v")
    mmv = mmrow[:, :, :].rearrange("o i (a b) -> o i a b", a=NEXP)
    b2_ = 8
    while b2_ >= 1:
        g.w(vec, "c_v")
        vec.tensor_tensor(mmv[:, 0, :, 0:b2_], mmv[:, 0, :, 0:b2_],
                          mmv[:, 0, :, b2_:2 * b2_], ALU.max)
        g.inc(vec.tensor_tensor(mmv[:, 1, :, 0:b2_], mmv[:, 1, :, 0:b2_],
                                mmv[:, 1, :, b2_:2 * b2_], ALU.min), "c_v")
        b2_ //= 2
    g.w(vec, "c_v")
    g.inc(vec.tensor_copy(tvx[:, :, :].rearrange("o i a -> o i a ()"),
                          mmv[:, :, :, 0:1]), "c_v")
    g.w(tn, "c_v")
    tn.transpose(p_h2[0:NEXP, 0:1], tvx[0:1, 0, :], ident_f[0:1, 0:1])
    g.inc(tn.transpose(p_h2[0:NEXP, 1:2], tvx[0:1, 1, :], ident_f[0:1, 0:1]), "c_t")
    g.w(vec, "c_t")
    vec.tensor_copy(bst[:, 1:2], p_h2[0:NEXP, 0:1])
    g.inc(vec.tensor_copy(bst[:, 6:7], p_h2[0:NEXP, 1:2]), "c_v")
    g.w(sc, "c_v")
    g.inc(sc.activation(bst[:, 7:8], bst[:, 6:7], AF.Abs), "c_s")
    g.w(vec, "c_s")
    g.inc(vec.tensor_scalar(bst[:, 7:8], bst[:, 7:8], 2.0 ** -10, 1.0,
                            ALU.mult, ALU.add), "c_v")
    g.w(vec, "c_v")
    g.inc(vec.tensor_tensor(bst[:, 0:1], bst[:, 6:7], bst[:, 7:8], ALU.subtract),
          "c_v")
    for it in range(32):
        g.w(vec, "c_v", "c_s")
        g.inc(vec.tensor_tensor(bst[:, 2:3], bst[:, 0:1], bst[:, 1:2], ALU.add), "c_v")
        g.w(vec, "c_v")
        g.inc(vec.tensor_scalar(bst[:, 2:3], bst[:, 2:3], 0.5, None, ALU.mult), "c_v")
        g.w(vec, "c_v")
        vec.tensor_scalar(bst[:, 3:4], bst[:, 2:3], -1.0, None, ALU.mult)
        vec.tensor_tensor(bst[:, 5:6], bst[:, 2:3], bst[:, 0:1], ALU.subtract)
        g.inc(vec.tensor_tensor(bst[:, 6:7], bst[:, 1:2], bst[:, 2:3], ALU.subtract),
              "c_v")
        g.w(tn, "c_v")
        g.inc(tn.matmul(p_h1[:, 0:1], m8x128[:], bst[:, 3:4], start=True, stop=True),
              "c_t")
        g.w(vec, "c_t")
        g.inc(vec.tensor_copy(nmid_sb[:], p_h1[:, 0:1]), "c_v")
        g.w(sc, "c_v")
        g.inc(sc.activation(outacc.bitcast(BF16)[:, 0, 0:FV], Vv[:], AF.Sign,
                            bias=nmid_sb[:, 0:1], accum_out=cnt[:, 0:1]), "c_s")
        g.w(tn, "c_s")
        g.inc(tn.matmul(p_h2[0:NEXP, 0:1], m128x8[:], cnt[:], start=True, stop=True),
              "c_t")
        g.w(vec, "c_t")
        g.inc(vec.tensor_scalar(bst[:, 4:5], p_h2[0:NEXP, 0:1], -(TOT / 2.0), None,
                                ALU.is_ge), "c_v")
        g.w(vec, "c_v")
        vec.tensor_tensor(bst[:, 5:6], bst[:, 5:6], bst[:, 4:5], ALU.mult)
        g.inc(vec.tensor_tensor(bst[:, 6:7], bst[:, 6:7], bst[:, 4:5], ALU.mult),
              "c_v")
        g.w(vec, "c_v")
        vec.tensor_tensor(bst[:, 0:1], bst[:, 0:1], bst[:, 5:6], ALU.add)
        g.inc(vec.tensor_tensor(bst[:, 1:2], bst[:, 2:3], bst[:, 6:7], ALU.add),
              "c_v")
    g.w(tn, "c_v")
    g.inc(tn.transpose(p_h2[0:1, 0:NEXP], bst[:, 1:2], ident_f[0:NEXP, 0:NEXP]),
          "c_t")
    g.w(vec, "c_t")
    g.inc(vec.tensor_copy(throw[:], p_h2[0:1, 0:NEXP]), "c_v")
    if upto == "thr":
        g.w(gp, "s_gst")
        g.w(sy, "s_st")
        g.w(vec, "c_v")
        nc.finalize()
        return nc
    g.w(gp, "c_v", "c_g")
    ins = None
    for e in range(NEXP):
        ins = gp.partition_broadcast(thrb[:, e:e + 1], throw[0:1, e:e + 1], 16)
    g.inc(ins, "c_g")
    for e in range(NEXP):
        g.w(sy, "c_v", "c_g")
        ldma(sy, cand[:], cc_in.ap()[e:e + 1, :].rearrange("o (f p) -> (o p) f", p=16),
             "s_ld", allow_slow_non_contiguous=True)
        g.w(vec, "s_ld")
        g.inc(vec.tensor_scalar(mask[:], cand[:], thrb[:, e:e + 1], None, ALU.is_ge), "c_v")
        g.w(vec, "c_v")
        g.inc(vec.scalar_tensor_tensor(cand[:], iot2[:], 1.0, mask[:], ALU.add,
                                       ALU.mult), "c_v")
        g.w(vec, "c_v")
        g.inc(vec.tensor_scalar(cand[:], cand[:], 1.0, None, ALU.subtract), "c_v")
        g.w(vec, "c_v")
        g.inc(vec.tensor_copy(candi[:, e, :], cand[:]), "c_v")
    for e in range(NEXP):
        g.w(gp, "c_v")
        g.inc(gp.sparse_gather(sgo[:], candi[:, e, :], num_found=nf[0:1, e:e + 1]), "c_g")
        g.w(vec, "c_g")
        g.inc(vec.tensor_copy(nff[:], nf[0:1, e:e + 1]), "c_v")
        g.w(gp, "c_v")
        g.inc(gp.partition_broadcast(nfb[:], nff[:], 16), "c_g")
        g.w(vec, "c_g")
        vec.tensor_scalar(vmask[:], slot16[:], nfb[:], None, ALU.is_lt)
        g.inc(vec.tensor_copy(idxf[:], sgo[:, 0:nsel // 16]), "c_v")
        g.w(vec, "c_v")
        g.inc(vec.tensor_tensor(idxf[:], idxf[:], vmask[:], ALU.mult), "c_v")
        g.w(vec, "c_v")
        g.inc(vec.tensor_scalar(vmask[:], vmask[:], 1.0, None, ALU.subtract), "c_v")
        g.w(vec, "c_v")
        g.inc(vec.tensor_scalar(vmask[:], vmask[:], -float(DUMMY), None, ALU.mult), "c_v")
        g.w(vec, "c_v")
        g.inc(vec.tensor_tensor(idxf[:], idxf[:], vmask[:], ALU.add), "c_v")
        g.w(vec, "c_v")
        g.inc(vec.tensor_copy(idxb[0:16, e, :], idxf[:]), "c_v")
        g.w(gp, "c_v")
        g.inc(gp.dma_start(idxb[16:32, e, :], idxb[0:16, e, :]), "s_gst", 16)

    if upto == "sg":
        g.w(gp, "s_gst")
        g.w(sy, "s_st")
        g.w(vec, "c_v")
        nc.finalize()
        return nc

    w1v = w1.rearrange("e (c p) f -> e p c f", p=128)
    w2v = w2.rearrange("e (c p) d -> e p c d", p=128)
    for e in range(NEXP):
        g.w(gp, "s_gst", "c_t", "c_s", "s_gat")
        for cq in range(nsel // 128):
            bufq = cq % 2
            g.w(gp, "c_t")
            g.inc(gp.dma_gather(gat_ch[:, bufq:bufq + 1, :], h2e.ap()[:],
                                idxb[:, e, 8 * cq:8 * cq + 8], 128, 128, DE),
                  "s_gat", 16)
            g.w(vec, "s_gat")
            g.inc(vec.tensor_copy(gcol[:, cq:cq + 1],
                                  gat_ch[:, bufq, DM + e:DM + e + 1]), "c_v")
            g.w(tn, "s_gat")
            four_slot = 2 * nsel >= 2048
            for dc in range(8):
                sl = 1024 * ((dc % 4) // 2) if four_slot else 0
                pt = (p_h1 if dc % 2 == 0 else p_h2).bitcast(BF16)
                g.wl(tn, "c_v", 3 if four_slot else 1)
                g.inc(tn.transpose(pt[:, sl:sl + 128],
                                   gat_ch[:, bufq, dc * 128:(dc + 1) * 128],
                                   ident_b[:]), "c_t")
                g.w(vec, "c_t")
                g.inc(vec.tensor_copy(gatF[:, dc, cq * 128:(cq + 1) * 128],
                                      pt[:, sl:sl + 128]), "c_v")
        g.w(sy, "c_v")
        ldma(sy, b1c[:], b1[e].rearrange("o (t p) -> (o p) t", p=128), "s_ld", allow_slow_non_contiguous=True)
        ldma(sy, b2row[:], b2[e], "s_ld")
        g.w(tn, "c_v", "s_ld")
        for hh in range(2):
            g.inc(tn.matmul(p_o[:, hh * 512:(hh + 1) * 512], ones_f[:],
                            b2row[0:1, hh * 512:(hh + 1) * 512], start=True, stop=True), "c_t")
        g.w(vec, "c_t")
        g.inc(vec.tensor_copy(b2g[:], p_o[:, 0:1024]), "c_v")
        g.w(vec, "c_v")
        g.inc(vec.tensor_tensor(b2g[:], b2g[:], GB[:], ALU.mult), "c_v")
        LAG_T = 4 + nsel // 128
        for q in range(8):
            qb = q % 2
            ff0 = q * (FF // 8)
            if q < 2:
                g.w(gp, "c_t", "c_s")
                g.w(sy, "c_t", "c_s")
            else:
                g.wl(gp, "c_t", LAG_T)
                g.wl(gp, "c_s", 4)
                g.wl(sy, "c_t", LAG_T)
                g.wl(sy, "c_s", 4)
            ldma(gp, w1q[:, qb], w1v[e][:, :, ff0:ff0 + FF // 8], "s_gld")
            ldma(gp, w2q[:, qb], w2v[e][:, 4 * q:4 * q + 4, :], "s_gld")
            g.w(tn, "s_gld", "s_ld", "s_gat", "c_v")
            for ffc in range(4):
                ph = p_h1 if ffc % 2 == 0 else p_h2
                g.wl(tn, "c_s", 1)
                ins = None
                for s0 in range(0, nsel, 512):
                    sw = min(512, nsel - s0)
                    for dc in range(8):
                        ins = tn.matmul(ph[:, s0:s0 + sw],
                                        w1q[:, qb, dc, ffc * 128:(ffc + 1) * 128],
                                        gatF[:, dc, s0:s0 + sw],
                                        start=(dc == 0), stop=(dc == 7))
                g.inc(ins, "c_t")
                g.w(sc, "c_t")
                g.inc(sc.activation(hidb[:, qb, ffc, :], ph[:, 0:nsel], AF.Gelu,
                                    bias=b1c[:, 4 * q + ffc:4 * q + ffc + 1]), "c_s")
            g.w(tn, "c_s")
            for sct in range(nsel // 128):
                g.w(tn, "c_v")
                ins = None
                for ffc in range(4):
                    for hh in range(2):
                        ins = tn.matmul(p_o[:, hh * 512:(hh + 1) * 512],
                                        hidb[:, qb, ffc, sct * 128:(sct + 1) * 128],
                                        w2q[:, qb, ffc, hh * 512:(hh + 1) * 512],
                                        start=(ffc == 0), stop=(ffc == 3))
                g.inc(ins, "c_t")
                g.w(vec, "c_t")
                if q == 0:
                    g.inc(vec.tensor_copy(outacc[:, sct, :], p_o[:, 0:1024]), "c_v")
                else:
                    g.inc(vec.tensor_tensor(outacc[:, sct, :], outacc[:, sct, :],
                                            p_o[:, 0:1024], ALU.add), "c_v")
        g.w(vec, "c_v")
        for sct in range(nsel // 128):
            vec.tensor_tensor(outacc[:, sct, :], outacc[:, sct, :], GB[:], ALU.mult)
        g.inc(vec.engine_nop(), "c_v")
        g.w(vec, "c_v")
        for sct in range(nsel // 128):
            vec.tensor_tensor(outacc[:, sct, :], outacc[:, sct, :], b2g[:], ALU.add)
        g.inc(vec.engine_nop(), "c_v")
        g.w(vec, "c_v")
        for sct in range(nsel // 128):
            vec.tensor_scalar(outacc[:, sct, :], outacc[:, sct, :], gcol[:, sct:sct + 1],
                              None, ALU.mult)
        g.inc(vec.engine_nop(), "c_v")
        g.w(gp, "c_v", "s_gat", "s_st")
        if upto == "noscat":
            g.inc(gp.dma_start(scat_dbg.ap()[:], outacc[:]), "s_gat", 16)
        else:
            for cq in range(nsel // 128):
                g.inc(gp.dma_scatter_add(out.ap()[:], outacc[:, cq:cq + 1, :],
                                         idxb[:, e, 8 * cq:8 * cq + 8], 128,
                                         128, DM), "s_gat", 16)

    g.w(gp, "s_gat")
    g.w(sy, "s_st")
    nc.finalize()
    return nc


_CACHE = {}


def _get_nc(T, ncores, nsel, upto):
    key = (T, ncores, nsel, upto)
    if key not in _CACHE:
        _CACHE[key] = build(T, ncores, nsel, upto)
    return _CACHE[key]


def kernel(L0_attn_u, L0_attn_in_proj_w, L0_attn_conv_w, L0_attn_conv_b,
           L0_attn_dt_bias, L0_attn_A_log, L0_attn_Dp, L0_attn_norm_w,
           L0_attn_out_proj_w, L0_ffn_Wg, L0_ffn_W1, L0_ffn_b1, L0_ffn_W2,
           L0_ffn_b2, L0_c, L0_wmod):
    B, T, _ = L0_attn_u.shape
    ncores = B
    nsel = 1280 if T * ncores >= 32768 else max(128, ((T * ncores // 4) // ncores * 2 + 127) // 128 * 128)
    nc = _get_nc(T, ncores, nsel, globals().get("STAGE", "full"))
    A = np.ascontiguousarray
    f32 = np.float32
    shared = {
        "w_inT": A(np.asarray(L0_attn_in_proj_w, f32).T),
        "conv_w": A(np.asarray(L0_attn_conv_w, f32)),
        "conv_b": A(np.asarray(L0_attn_conv_b, f32).reshape(1, CONVD)),
        "dt_b_c": A(np.asarray(L0_attn_dt_bias, f32).reshape(NH, 1)),
        "a_log_c": A(np.asarray(L0_attn_A_log, f32).reshape(NH, 1)),
        "d_p_c": A(np.asarray(L0_attn_Dp, f32).reshape(NH, 1)),
        "norm_w": A(np.asarray(L0_attn_norm_w, f32).reshape(1, E)),
        "w_outT": A(np.asarray(L0_attn_out_proj_w, f32).T),
        "wg": A(np.asarray(L0_ffn_Wg, f32)),
        "w1": A(np.asarray(L0_ffn_W1, f32)),
        "b1": A(np.asarray(L0_ffn_b1, f32).reshape(NEXP, 1, FF)),
        "w2": A(np.asarray(L0_ffn_W2, f32)),
        "b2": A(np.asarray(L0_ffn_b2, f32).reshape(NEXP, 1, DM)),
        "wmod": A(np.asarray(L0_wmod, f32)),
    }
    uu = np.asarray(L0_attn_u, f32)
    cc = np.asarray(L0_c, f32)
    in_maps = []
    for b in range(ncores):
        m = dict(shared)
        m["u"] = A(uu[b])
        m["c_row"] = A(cc[b].reshape(1, DM))
        in_maps.append(m)
    res = run_bass_kernel_spmd(nc, in_maps, core_ids=list(range(ncores)))
    return np.stack([np.asarray(r["out"])[0:T] for r in res.results])


# version 10
# speedup vs baseline: 1.0135x; 1.0135x over previous
from contextlib import ExitStack

import numpy as np
import ml_dtypes

import concourse.bacc as bacc
import concourse.mybir as mybir
from concourse.bass_utils import run_bass_kernel_spmd

F32 = mybir.dt.float32
BF16 = mybir.dt.bfloat16
I16 = mybir.dt.int16
U32 = mybir.dt.uint32
ALU = mybir.AluOpType
AF = mybir.ActivationFunctionType

DM = 1024
E = 2048
NH = 16
HD = 128
CHK = 256
DIN = 4240
CONVD = 2176
FF = 4096
NEXP = 8
EPS = 1e-5
DE = DM + 128


class P:
    def __init__(self, nc, stack):
        self.nc = nc
        self.stack = stack
        self.s = {}

    def sem(self, name):
        self.s[name] = [self.stack.enter_context(self.nc.semaphore(name)), 0]

    def inc(self, instr, name, k=1):
        e = self.s[name]
        e[1] += k
        return instr.then_inc(e[0], k)

    def w(self, eng, *names):
        for n in names:
            eng.wait_ge(self.s[n][0], self.s[n][1])

    def wl(self, eng, name, lag):
        e = self.s[name]
        if e[1] - lag > 0:
            eng.wait_ge(e[0], e[1] - lag)


def build(T, ncores, nsel, upto="full"):
    NCHUNK = T // CHK
    TOT = T * ncores
    KSEL = TOT // 4
    DUMMY = T

    nc = bacc.Bacc("TRN2", detect_race_conditions=False)
    big = ExitStack()
    g = P(nc, big)
    SB = lambda n, sh, dt: big.enter_context(nc.sbuf_tensor(n, sh, dt))

    DT = nc.dram_tensor
    u = DT("u", [T, DM], F32, kind="ExternalInput").ap()
    w_inT = DT("w_inT", [DM, DIN], F32, kind="ExternalInput").ap()
    conv_w = DT("conv_w", [CONVD, 4], F32, kind="ExternalInput").ap()
    conv_b = DT("conv_b", [1, CONVD], F32, kind="ExternalInput").ap()
    dt_b_c = DT("dt_b_c", [NH, 1], F32, kind="ExternalInput").ap()
    a_log_c = DT("a_log_c", [NH, 1], F32, kind="ExternalInput").ap()
    d_p_c = DT("d_p_c", [NH, 1], F32, kind="ExternalInput").ap()
    norm_w = DT("norm_w", [1, E], F32, kind="ExternalInput").ap()
    w_outT = DT("w_outT", [E, DM], F32, kind="ExternalInput").ap()
    wg = DT("wg", [DM, NEXP], F32, kind="ExternalInput").ap()
    w1 = DT("w1", [NEXP, DM, FF], F32, kind="ExternalInput").ap()
    b1 = DT("b1", [NEXP, 1, FF], F32, kind="ExternalInput").ap()
    w2 = DT("w2", [NEXP, FF, DM], F32, kind="ExternalInput").ap()
    b2 = DT("b2", [NEXP, 1, DM], F32, kind="ExternalInput").ap()
    c_row = DT("c_row", [1, DM], F32, kind="ExternalInput").ap()
    wmod = DT("wmod", [DM, 6 * DM], F32, kind="ExternalInput").ap()

    out = DT("out", [T + 1, DM], F32, kind="ExternalOutput")

    h2e = DT("h2e", [T + 1, DE], BF16, kind="Internal")
    winT_b = DT("winT_b", [128, 8, DIN], BF16, kind="Internal")
    woutT_b = DT("woutT_b", [128, 16, DM], BF16, kind="Internal")
    cc_in = DT("cc_in", [NEXP, T], F32, kind="Internal")
    cc_out = DT("cc_out", [ncores, NEXP, T], F32, kind="Internal")
    scat_dbg = DT("scat_dbg", [128, nsel // 128, DM], F32, kind="Internal")

    for s in ("s_ld", "s_st", "s_gld", "s_gst", "s_gat",
              "c_t", "c_v", "c_s", "c_g", "c_cc", "c_w"):
        g.sem(s)

    tn, vec, sc, gp, sy = nc.tensor, nc.vector, nc.scalar, nc.gpsimd, nc.sync

    def ldma(eng, dst, src, sem, **kw):
        return g.inc(eng.dma_start(dst, src, **kw), sem, 16)

    def mmacc(pm, pairs, sem="c_t"):
        n = len(pairs)
        ins = None
        for i, (l, r) in enumerate(pairs):
            ins = tn.matmul(pm, l, r, start=(i == 0), stop=(i == n - 1))
        return g.inc(ins, sem)

    ident_b = SB("ident_b", [128, 128], BF16)
    ident_f = SB("ident_f", [128, 128], F32)
    ones_f = SB("ones_f", [1, 128], F32)
    onesc_b = SB("onesc_b", [128, 1], BF16)
    trimask = SB("trimask", [128, 2, CHK], F32)
    smalls = SB("smalls", [NH, 8], F32)
    abc = SB("abc", [128, NH], F32)
    dbc = SB("dbc", [128, NH], F32)
    nwc = SB("nwc", [128, E // 128], F32)
    cwx = SB("cwx", [128, 16, 4], F32)
    cbx = SB("cbx", [128, 16], F32)
    cwB = SB("cwB", [64, 4], F32)
    cbB = SB("cbB", [64, 1], F32)
    cwC = SB("cwC", [64, 4], F32)
    cbC = SB("cbC", [64, 1], F32)
    CA1 = SB("CA1", [128, DM], F32)
    SA = SB("SA", [128, DM], F32)
    CB1 = SB("CB1", [128, DM], F32)
    SBt = SB("SBt", [128, DM], F32)
    GA = SB("GA", [128, DM], F32)
    GB = SB("GB", [128, DM], F32)
    ssd_S = SB("ssd_S", [64, E], F32)
    ssd_Sb = SB("ssd_Sb", [64, E], BF16)
    logit_sb = SB("logit_sb", [NEXP, CHK], F32)
    wgt = SB("wgt", [128, 8, NEXP], F32)
    zrow_b = SB("zrow_b", [1, DE], BF16)
    konst = SB("konst", [128, 4], F32)

    su = ExitStack()
    SU = lambda n, sh, dt: su.enter_context(nc.sbuf_tensor(n, sh, dt))
    PSU = lambda n, sh: su.enter_context(nc.psum_tensor(n, sh, F32))
    mrow = SU("mrow", [1, 6 * DM], F32)
    crow = SU("crow", [1, DM], F32)
    ccol = SU("ccol", [128, 8], F32)
    wstg = SU("wstg", [128, 2, 4480], BF16)
    t2 = SU("t2", [1, 4 * DM], F32)
    t3 = SU("t3", [1, 2 * DM], F32)
    p_u1 = PSU("p_u1", [128, 512])
    p_u2 = PSU("p_u2", [1, 512])

    vec.memset(konst[:, 0:1], EPS)
    vec.memset(konst[:, 1:2], -0.5)
    vec.memset(konst[:, 2:3], -1.0)
    vec.memset(konst[:, 3:4], 1.0)
    nc.const_aps.register(F32, EPS, konst[:, 0:1])
    nc.const_aps.register(F32, -0.5, konst[:, 1:2])
    nc.const_aps.register(F32, -1.0, konst[:, 2:3])
    nc.const_aps.register(F32, 1.0, konst[:, 3:4])
    vec.memset(ident_b[:], 1.0)
    vec.memset(ident_f[:], 1.0)
    vec.memset(ones_f[:], 1.0)
    vec.memset(onesc_b[:], 1.0)
    vec.memset(zrow_b[:], 0.0)
    g.inc(vec.memset(trimask[:], 0.0), "c_v")
    g.w(gp, "c_v")
    gp.affine_select(ident_b[:], ident_b[:], [[-1, 128]], ALU.is_equal, 0.0,
                     base=0, channel_multiplier=1)
    gp.affine_select(ident_f[:], ident_f[:], [[-1, 128]], ALU.is_equal, 0.0,
                     base=0, channel_multiplier=1)
    for kt in range(2):
        g.inc(gp.affine_select(trimask[:, kt, :], trimask[:, kt, :], [[1, CHK]],
                               ALU.is_ge, -1e30, base=-128 * kt,
                               channel_multiplier=-1), "c_g")

    ldma(sy, smalls[:, 0:1], dt_b_c, "s_ld")
    ldma(sy, smalls[:, 3:4], a_log_c, "s_ld")
    ldma(sy, smalls[:, 2:3], d_p_c, "s_ld")
    ldma(sy, nwc[:], norm_w.rearrange("o (t p) -> (o p) t", p=128), "s_ld", allow_slow_non_contiguous=True)
    ldma(sy, cwx[:], conv_w[0:2048].rearrange("(t p) w -> p t w", p=128), "s_ld")
    ldma(sy, cbx[:], conv_b[0:1, 0:2048].rearrange("o (t p) -> (o p) t", p=128), "s_ld", allow_slow_non_contiguous=True)
    ldma(sy, cwB[:], conv_w[2048:2112], "s_ld")
    ldma(sy, cbB[:], conv_b[0:1, 2048:2112].rearrange("o (a p) -> (o p) a", p=64), "s_ld")
    ldma(sy, cwC[:], conv_w[2112:2176], "s_ld")
    ldma(sy, cbC[:], conv_b[0:1, 2112:2176].rearrange("o (a p) -> (o p) a", p=64), "s_ld")
    ldma(sy, crow[:], c_row, "s_ld")
    ldma(sy, wgt[:], wg.rearrange("(c p) e -> p c e", p=128), "s_ld")
    g.w(gp, "c_v")
    g.inc(gp.dma_start(h2e.ap()[T:T + 1, :], zrow_b[:]), "s_gst", 16)

    g.w(sc, "s_ld")
    g.inc(sc.activation(smalls[:, 4:5], smalls[:, 3:4], AF.Exp), "c_s")
    g.w(vec, "c_s")
    g.inc(vec.tensor_scalar(smalls[:, 1:2], smalls[:, 4:5], -1.0, None, ALU.mult), "c_v")
    g.w(tn, "c_v")
    tn.transpose(p_u1[0:1, 0:NH], smalls[:, 1:2], ident_f[0:NH, 0:NH])
    g.inc(tn.transpose(p_u1[0:1, 64:64 + NH], smalls[:, 2:3], ident_f[0:NH, 0:NH]), "c_t")
    g.w(vec, "c_t")
    vec.tensor_copy(t3[0:1, 0:NH], p_u1[0:1, 0:NH])
    g.inc(vec.tensor_copy(t3[0:1, 64:64 + NH], p_u1[0:1, 64:64 + NH]), "c_v")
    g.w(tn, "c_v")
    tn.matmul(p_u1[:, 0:NH], ones_f[:], t3[0:1, 0:NH], start=True, stop=True)
    g.inc(tn.matmul(p_u1[:, 64:64 + NH], ones_f[:], t3[0:1, 64:64 + NH],
                    start=True, stop=True), "c_t")
    g.w(vec, "c_t")
    vec.tensor_copy(abc[:], p_u1[:, 0:NH])
    g.inc(vec.tensor_copy(dbc[:], p_u1[:, 64:64 + NH]), "c_v")

    g.inc(sc.activation(crow[:], crow[:], AF.Silu), "c_s")
    g.w(tn, "c_s")
    ins = None
    for dc in range(8):
        ins = tn.transpose(p_u1[:, 32 + dc:32 + dc + 1], crow[0:1, dc * 128:(dc + 1) * 128],
                           ident_f[0:1, 0:1])
    g.inc(ins, "c_t")
    g.w(vec, "c_t")
    g.inc(vec.tensor_copy(ccol[:], p_u1[:, 32:40]), "c_v")
    wmv = wmod.rearrange("(c p) n -> p c n", p=128)
    wstg_f = wstg.bitcast(F32)
    for nx in range(24):
        buf = wstg_f[:, nx % 2, 0:2048].rearrange("p (c n) -> p c n", c=8)
        if nx >= 2:
            g.w(gp, "c_t")
        ldma(gp, buf, wmv[:, :, nx * 256:(nx + 1) * 256], "s_gld")
        g.w(tn, "s_gld", "c_v")
        mmacc(p_u2[0:1, 0:256], [(ccol[:, dc:dc + 1], buf[:, dc, :]) for dc in range(8)])
        g.w(vec, "c_t")
        g.inc(vec.tensor_copy(mrow[0:1, nx * 256:(nx + 1) * 256], p_u2[0:1, 0:256]), "c_v")
    g.w(sc, "c_v")
    sc.activation(t2[0:1, 0:DM], mrow[0:1, 2 * DM:3 * DM], AF.Abs)
    g.inc(sc.activation(t2[0:1, DM:2 * DM], mrow[0:1, 5 * DM:6 * DM], AF.Abs), "c_s")
    g.w(vec, "c_s")
    vec.tensor_scalar(t3[0:1, 0:DM], mrow[0:1, 2 * DM:3 * DM], 0.0, None, ALU.is_ge)
    g.inc(vec.tensor_scalar(t3[0:1, DM:2 * DM], mrow[0:1, 5 * DM:6 * DM], 0.0, None,
                            ALU.is_ge), "c_v")
    vec.tensor_scalar(t2[0:1, 0:2 * DM], t2[0:1, 0:2 * DM], 0.25, None, ALU.max)
    g.inc(vec.tensor_scalar(t3[0:1, 0:2 * DM], t3[0:1, 0:2 * DM], 2.0, -1.0,
                            ALU.mult, ALU.add), "c_v")
    g.w(vec, "c_v")
    g.inc(vec.tensor_tensor(t2[0:1, 0:2 * DM], t2[0:1, 0:2 * DM], t3[0:1, 0:2 * DM],
                            ALU.mult), "c_v")
    vec.tensor_scalar(t2[0:1, 2 * DM:3 * DM], mrow[0:1, 1 * DM:2 * DM], 1.0, None, ALU.add)
    g.inc(vec.tensor_scalar(t2[0:1, 3 * DM:4 * DM], mrow[0:1, 4 * DM:5 * DM], 1.0, None,
                            ALU.add), "c_v")
    g.w(tn, "c_v")
    for dst, src in ((CA1, t2[0:1, 2 * DM:3 * DM]), (SA, mrow[0:1, 0:DM]),
                     (CB1, t2[0:1, 3 * DM:4 * DM]), (SBt, mrow[0:1, 3 * DM:4 * DM]),
                     (GA, t2[0:1, 0:DM]), (GB, t2[0:1, DM:2 * DM])):
        for hh in range(2):
            g.w(tn, "c_v")
            g.inc(tn.matmul(p_u1[:, 0:512], ones_f[:],
                            src[0:1, hh * 512:(hh + 1) * 512], start=True, stop=True), "c_t")
            g.w(vec, "c_t")
            g.inc(vec.tensor_copy(dst[:, hh * 512:(hh + 1) * 512], p_u1[:, 0:512]), "c_v")

    winTv = w_inT.rearrange("(c p) f -> p c f", p=128)
    g.w(gp, "c_t")
    for i in range(8):
        f0 = i * 530
        buf = wstg[:, i % 2, 0:4240].rearrange("p (c f) -> p c f", c=8)
        if i >= 2:
            g.w(gp, "s_gst")
        ldma(gp, buf, winTv[:, :, f0:f0 + 530], "s_gld")
        g.w(gp, "s_gld")
        g.inc(gp.dma_start(winT_b.ap()[:, :, f0:f0 + 530], buf), "s_gst", 16)
    woutTv = w_outT.rearrange("(c p) f -> p c f", p=128)
    for i in range(4):
        buf = wstg[:, i % 2, 0:4096].rearrange("p (c f) -> p c f", c=4)
        g.w(gp, "s_gst")
        ldma(gp, buf, woutTv[:, i * 4:(i + 1) * 4, :], "s_gld")
        g.w(gp, "s_gld")
        g.inc(gp.dma_start(woutT_b.ap()[:, i * 4:(i + 1) * 4, :], buf), "s_gst", 16)
    vec.memset(ssd_S[:], 0.0)
    g.inc(vec.memset(ssd_Sb[:], 0.0), "c_v")
    g.w(gp, "c_v")
    g.w(sy, "s_gst")
    su.close()

    mp = ExitStack()
    M = lambda n, sh, dt: mp.enter_context(nc.sbuf_tensor(n, sh, dt))
    PM = lambda n, sh: mp.enter_context(nc.psum_tensor(n, sh, F32))
    x_tm = M("x_tm", [128, 2, DM], F32)
    stat = M("stat", [128, 2, 8], F32)
    h_f32 = M("h_f32", [128, 2, DM], F32)
    h_tm = M("h_tm", [128, 2, DM], BF16)
    h_fm = M("h_fm", [128, 8, CHK], BF16)
    winq = M("winq", [128, 2, 8, 640], BF16)
    woth = M("woth", [128, 8, DM], BF16)
    z_b = M("z_b", [128, 16, CHK], BF16)
    xraw = M("xraw", [128, 16, CHK + 3], F32)
    Braw = M("Braw", [64, CHK + 3], F32)
    Craw = M("Craw", [64, CHK + 3], F32)
    xpost = M("xpost", [128, 16, CHK], BF16)
    Bpost = M("Bpost", [64, CHK], BF16)
    Cpost = M("Cpost", [64, CHK], BF16)
    Cpf = M("Cpf", [64, CHK], F32)
    dt_sb = M("dt_sb", [NH, CHK], F32)
    a_rm = M("a_rm", [NH, CHK], F32)
    acum_rm = M("acum_rm", [NH, CHK], F32)
    acfl = M("acfl", [1, NH * CHK], F32)
    ce_f = M("ce_f", [64, 2, CHK], F32)
    aclrow = M("aclrow", [1, NH], F32)
    cdrow = M("cdrow", [1, NH], F32)
    dtc_tm = M("dtc_tm", [128, 2, NH], F32)
    acum_tm = M("acum_tm", [128, 2, NH], F32)
    w_tm = M("w_tm", [128, 2, NH], F32)
    cdcol = M("cdcol", [64, NH], F32)
    xcT = M("xcT", [128, 2, E], BF16)
    wx_b = M("wx_b", [128, 2, HD], BF16)
    Bt = M("Bt", [128, 2, 64], BF16)
    scot = M("scot", [128, 2, CHK], F32)
    dcy = M("dcy", [128, 2, CHK], F32)
    mt_b = M("mt_b", [128, 2, CHK], BF16)
    ce_b = M("ce_b", [64, 2, CHK], BF16)
    yg_s = M("yg_s", [128, 2, CHK], F32)
    ysq_b = M("ysq_b", [128, 2, CHK], BF16)
    rms_r = M("rms_r", [1, CHK], F32)
    rbc = M("rbc", [128, CHK], F32)
    y_b = M("y_b", [128, 16, CHK], BF16)
    x2_tm = M("x2_tm", [128, 2, DM], F32)
    
    h2fm = M("h2fm", [128, 8, 128], F32)
    h2es = M("h2es", [128, 2, DE], BF16)
    p_a = PM("p_a", [128, 512])
    p_b = PM("p_b", [128, 512])
    p_c = PM("p_c", [128, 512])
    p_d = PM("p_d", [128, 512])
    p_st = PM("p_st", [64, 1024])
    p_sm = PM("p_sm", [128, 512])
    p_lg = PM("p_lg", [8, 512])

    vec.memset(h2es[:], 0.0)
    vec.memset(xraw[:], 0.0)
    vec.memset(Braw[:], 0.0)
    g.inc(vec.memset(Craw[:], 0.0), "c_v")

    WGRP = [(0, 5), (5, 9), (9, 13), (13, 17), (17, 21), (21, 25), (25, 30), (30, 34)]
    uv = u.rearrange("(a p) d -> p a d", p=128)
    ov = out.ap()[0:T].rearrange("(a p) d -> p a d", p=128)
    h2ev = h2e.ap()[0:T].rearrange("(a p) d -> p a d", p=128)

    def ln_block(src2, dst2, Ctile, Stile, out_bf=None):
        g.w(sc, "c_v")
        for a in range(2):
            sc.activation(dst2[:, a, :], src2[:, a, :], AF.Copy, accum_out=stat[:, a, 0:1])
            g.inc(sc.activation(dst2[:, a, :], src2[:, a, :], AF.Square,
                                accum_out=stat[:, a, 1:2]), "c_s")
        g.w(vec, "c_s")
        for a in range(2):
            vec.tensor_scalar(stat[:, a, 2:3], stat[:, a, 0:1], 1.0 / DM, None, ALU.mult)
            vec.tensor_scalar(stat[:, a, 3:4], stat[:, a, 1:2], 1.0 / DM, None, ALU.mult)
        for a in range(2):
            vec.tensor_tensor(stat[:, a, 4:5], stat[:, a, 2:3], stat[:, a, 2:3], ALU.mult)
        for a in range(2):
            g.inc(vec.tensor_tensor(stat[:, a, 5:6], stat[:, a, 3:4], stat[:, a, 4:5],
                                    ALU.subtract), "c_v")
        g.w(sc, "c_v")
        for a in range(2):
            sc.activation(stat[:, a, 6:7], stat[:, a, 5:6], AF.Ln, bias=EPS)
        for a in range(2):
            g.inc(sc.activation(stat[:, a, 7:8], stat[:, a, 6:7], AF.Exp, scale=-0.5), "c_s")
        g.w(vec, "c_s")
        for a in range(2):
            vec.tensor_scalar(dst2[:, a, :], src2[:, a, :], stat[:, a, 2:3],
                              stat[:, a, 7:8], ALU.subtract, ALU.mult)
        for a in range(2):
            vec.tensor_tensor(dst2[:, a, :], dst2[:, a, :], Ctile[:], ALU.mult)
        for a in range(2):
            tgt = dst2[:, a, :] if out_bf is None else out_bf[:, a, :]
            g.inc(vec.tensor_tensor(tgt, dst2[:, a, :], Stile[:], ALU.add), "c_v")

    for ci in range(NCHUNK):
        par = ci % 2
        t0 = ci * CHK
        g.w(sy, "c_v")
        ldma(sy, x_tm[:], uv[:, 2 * ci:2 * ci + 2, :], "s_ld")
        g.w(sc, "s_ld")
        ln_block(x_tm, h_f32, CA1, SA, out_bf=h_tm)
        g.w(tn, "c_v")
        for a in range(2):
            for dc in range(8):
                pt = (p_c if (a * 8 + dc) % 2 == 0 else p_d).bitcast(BF16)
                g.wl(tn, "c_v", 1)
                g.inc(tn.transpose(pt[:, 0:128], h_tm[:, a, dc * 128:(dc + 1) * 128],
                                   ident_b[:]), "c_t")
                g.w(vec, "c_t")
                g.inc(vec.tensor_copy(h_fm[:, dc, a * 128:(a + 1) * 128], pt[:, 0:128]), "c_v")
        for g8, (ta, tb) in enumerate(WGRP):
            wb = winq[:, g8 % 2]
            f0 = ta * 128
            fn = min(tb * 128, DIN) - f0
            g.wl(gp, "c_w", 1)
            ldma(gp, wb[:, :, 0:fn], winT_b.ap()[:, :, f0:f0 + fn], "s_gld")
            g.w(tn, "s_gld")
            for t in range(ta, tb):
                off = t * 128 - f0
                g.w(tn, "c_v", "c_s")
                if t == 32:
                    for half in range(2):
                        pm = p_a if half == 0 else p_b
                        g.w(tn, "c_v")
                        mmacc(pm[0:64, 0:CHK],
                              [(wb[:, dc, off + 64 * half:off + 64 * half + 64],
                                h_fm[:, dc, :]) for dc in range(8)])
                        g.w(vec, "c_t")
                        raw = Braw if half == 0 else Craw
                        g.inc(vec.tensor_copy(raw[:, 3:3 + CHK], pm[0:64, 0:CHK]), "c_v")
                    continue
                prow = 128 if t < 33 else 16
                pm = p_a if t % 2 == 0 else p_b
                mmacc(pm[0:prow, 0:CHK],
                      [(wb[:, dc, off:off + prow], h_fm[:, dc, :]) for dc in range(8)])
                if t < 16:
                    g.w(sc, "c_t")
                    g.inc(sc.activation(z_b[:, t, :], pm[:, 0:CHK], AF.Silu), "c_s")
                elif t < 32:
                    g.w(vec, "c_t")
                    g.inc(vec.tensor_copy(xraw[:, t - 16, 3:3 + CHK], pm[:, 0:CHK]), "c_v")
                else:
                    g.w(vec, "c_t")
                    g.inc(vec.tensor_scalar(dt_sb[:], pm[0:NH, 0:CHK], smalls[:, 0:1],
                                            None, ALU.add), "c_v")
                    g.w(sc, "c_v")
                    g.inc(sc.activation(a_rm[:], dt_sb[:], AF.Abs), "c_s")
                    g.w(sc, "c_s")
                    g.inc(sc.activation(a_rm[:], a_rm[:], AF.Exp, scale=-1.0), "c_s")
                    g.w(sc, "c_s")
                    g.inc(sc.activation(a_rm[:], a_rm[:], AF.Ln, bias=1.0), "c_s")
                    g.w(sc, "c_s")
                    g.inc(sc.activation(acum_rm[:], dt_sb[:], AF.Relu), "c_s")
                    g.w(vec, "c_s")
                    g.inc(vec.tensor_tensor(dt_sb[:], acum_rm[:], a_rm[:], ALU.add), "c_v")
            g.inc(tn.nop(), "c_w")
        g.w(vec, "c_v", "c_s")
        for t in range(16):
            pm = p_c if t % 2 == 0 else p_d
            if t >= 2:
                g.w(vec, "c_s")
            vec.tensor_scalar(pm[:, 0:CHK], xraw[:, t, 0:CHK], cwx[:, t, 0:1],
                              cbx[:, t:t + 1], ALU.mult, ALU.add)
            for wv in range(1, 4):
                vec.scalar_tensor_tensor(pm[:, 0:CHK], xraw[:, t, wv:wv + CHK],
                                         cwx[:, t, wv:wv + 1], pm[:, 0:CHK],
                                         ALU.mult, ALU.add)
            g.inc(vec.tensor_copy(xraw[:, t, 0:3], xraw[:, t, CHK:CHK + 3]), "c_v")
            g.w(sc, "c_v")
            g.inc(sc.activation(xpost[:, t, :], pm[:, 0:CHK], AF.Silu), "c_s")
        for bc in range(2):
            raw, po, cw, cb = ((Braw, Bpost, cwB, cbB), (Craw, Cpost, cwC, cbC))[bc]
            pm = p_c if bc == 0 else p_d
            g.w(vec, "c_s")
            vec.tensor_scalar(pm[0:64, 0:CHK], raw[:, 0:CHK], cw[:, 0:1], cb[:],
                              ALU.mult, ALU.add)
            for wv in range(1, 4):
                vec.scalar_tensor_tensor(pm[0:64, 0:CHK], raw[:, wv:wv + CHK],
                                         cw[:, wv:wv + 1], pm[0:64, 0:CHK],
                                         ALU.mult, ALU.add)
            g.inc(vec.tensor_copy(raw[:, 0:3], raw[:, CHK:CHK + 3]), "c_v")
            g.w(sc, "c_v")
            g.inc(sc.activation(po[:], pm[0:64, 0:CHK], AF.Silu), "c_s")
            if bc == 1:
                g.inc(sc.activation(Cpf[:], pm[0:64, 0:CHK], AF.Silu), "c_s")
        g.w(vec, "c_s")
        g.inc(vec.tensor_scalar(a_rm[:], dt_sb[:], smalls[:, 1:2], None, ALU.mult), "c_v")
        g.w(vec, "c_v")
        g.inc(vec.tensor_tensor_scan(acum_rm[:], a_rm[:], a_rm[:], 0.0, ALU.add,
                                     ALU.bypass), "c_v")
        g.w(gp, "c_v")
        g.inc(gp.dma_start(acfl[:], acum_rm[:]), "s_gst", 16)
        g.w(tn, "c_v")
        for kt in range(2):
            if kt:
                g.w(tn, "c_v")
            tn.transpose(p_c[:, 0:NH], dt_sb[:, kt * 128:(kt + 1) * 128],
                         ident_f[0:NH, 0:NH])
            g.inc(tn.transpose(p_d[:, 0:NH], acum_rm[:, kt * 128:(kt + 1) * 128],
                               ident_f[0:NH, 0:NH]), "c_t")
            g.w(vec, "c_t")
            vec.tensor_copy(dtc_tm[:, kt, :], p_c[:, 0:NH])
            g.inc(vec.tensor_copy(acum_tm[:, kt, :], p_d[:, 0:NH]), "c_v")
        g.w(tn, "c_v")
        g.inc(tn.transpose(p_sm[0:1, 0:NH], acum_rm[:, CHK - 1:CHK],
                           ident_f[0:NH, 0:NH]), "c_t")
        g.w(vec, "c_t")
        g.inc(vec.tensor_copy(aclrow[:], p_sm[0:1, 0:NH]), "c_v")
        g.w(sc, "c_v")
        g.inc(sc.activation(cdrow[:], aclrow[:], AF.Exp), "c_s")
        g.w(tn, "c_v", "c_s")
        tn.matmul(p_sm[:, 0:NH], ones_f[:], aclrow[:], start=True, stop=True)
        g.inc(tn.matmul(p_sm[0:64, 64:64 + NH], ones_f[0:1, 0:64], cdrow[:],
                        start=True, stop=True), "c_t")
        g.w(vec, "c_t")
        g.inc(vec.tensor_copy(cdcol[:], p_sm[0:64, 64:64 + NH]), "c_v")
        for kt in range(2):
            vec.tensor_tensor(w_tm[:, kt, :], p_sm[:, 0:NH], acum_tm[:, kt, :],
                              ALU.subtract)
        g.inc(vec.engine_nop(), "c_v")
        g.w(sc, "c_v")
        g.inc(sc.activation(w_tm[:], w_tm[:], AF.Exp), "c_s")
        g.w(vec, "c_s")
        g.inc(vec.tensor_tensor(w_tm[:], w_tm[:], dtc_tm[:], ALU.mult), "c_v")
        g.w(tn, "c_v", "c_s")
        for kt in range(2):
            for t in range(16):
                pt = (p_c if t % 2 == 0 else p_d).bitcast(BF16)
                if kt == 1 and t == 0:
                    g.w(tn, "c_v")
                else:
                    g.wl(tn, "c_v", 1)
                g.inc(tn.transpose(pt[:, 0:128], xpost[:, t, kt * 128:(kt + 1) * 128],
                                   ident_b[:]), "c_t")
                g.w(vec, "c_t")
                g.inc(vec.tensor_copy(xcT[:, kt, t * 128:(t + 1) * 128], pt[:, 0:128]), "c_v")
            g.w(tn, "c_v")
            g.inc(tn.transpose(p_c.bitcast(BF16)[:, 0:64],
                               Bpost[:, kt * 128:(kt + 1) * 128], ident_b[0:64, 0:64]), "c_t")
            g.w(vec, "c_t")
            g.inc(vec.tensor_copy(Bt[:, kt, :], p_c.bitcast(BF16)[:, 0:64]), "c_v")
        g.w(tn, "c_v")
        for kt in range(2):
            pm = p_a if kt == 0 else p_b
            g.inc(tn.matmul(pm[:, 0:CHK], Bpost[:, kt * 128:(kt + 1) * 128], Cpost[:],
                            start=True, stop=True), "c_t")
            g.w(vec, "c_t")
            g.inc(vec.tensor_copy(scot[:, kt, :], pm[:, 0:CHK]), "c_v")
        for h in range(NH):
            hp = h % 2
            pmy = p_a if hp == 0 else p_b
            g.w(tn, "c_v", "c_s", "s_gst")
            ins = None
            for kt in range(2):
                ins = tn.matmul(p_sm[:, kt * CHK:(kt + 1) * CHK], ones_f[:],
                                acfl[0:1, h * CHK:(h + 1) * CHK], start=True, stop=True)
            g.inc(ins, "c_t")
            g.w(vec, "c_t")
            ins = None
            for kt in range(2):
                ins = vec.scalar_tensor_tensor(dcy[:, kt, :], p_sm[:, kt * CHK:(kt + 1) * CHK],
                                               acum_tm[:, kt, h:h + 1], trimask[:, kt, :],
                                               ALU.subtract, ALU.add)
            g.inc(ins, "c_v")
            g.w(sc, "c_v")
            sc.activation(ce_f[:, hp, :], p_sm[0:64, 0:CHK], AF.Exp)
            g.inc(sc.activation(dcy[:], dcy[:], AF.Exp), "c_s")
            g.w(vec, "c_s")
            g.inc(vec.tensor_tensor(ce_b[:, hp, :], ce_f[:, hp, :], Cpf[:], ALU.mult), "c_v")
            g.w(vec, "c_s")
            for kt in range(2):
                vec.tensor_tensor(dcy[:, kt, :], scot[:, kt, :], dcy[:, kt, :], ALU.mult)
            g.inc(vec.engine_nop(), "c_v")
            g.w(vec, "c_v")
            for kt in range(2):
                vec.tensor_scalar(mt_b[:, kt, :], dcy[:, kt, :], dtc_tm[:, kt, h:h + 1],
                                  None, ALU.mult)
            for kt in range(2):
                g.inc(vec.tensor_scalar(wx_b[:, kt, :], xcT[:, kt, h * HD:(h + 1) * HD],
                                        w_tm[:, kt, h:h + 1], None, ALU.mult), "c_v")
            g.w(tn, "c_v")
            for kt in range(2):
                tn.matmul(pmy[:, 0:CHK], xcT[:, kt, h * HD:(h + 1) * HD], mt_b[:, kt, :],
                          start=(kt == 0), stop=False)
            g.inc(tn.matmul(pmy[:, 0:CHK], ssd_Sb[:, h * HD:(h + 1) * HD], ce_b[:, hp, :],
                            start=False, stop=True), "c_t")
            ins = None
            for kt in range(2):
                ins = tn.matmul(p_st[:, (h % 8) * HD:(h % 8 + 1) * HD], Bt[:, kt, :],
                                wx_b[:, kt, :], start=(kt == 0), stop=(kt == 1))
            g.inc(ins, "c_t")
            g.w(vec, "c_t")
            vec.scalar_tensor_tensor(yg_s[:, hp, :], xpost[:, h, :], dbc[:, h:h + 1],
                                     pmy[:, 0:CHK], ALU.mult, ALU.add)
            g.inc(vec.engine_nop(), "c_v")
            g.w(vec, "c_v")
            g.inc(vec.tensor_tensor(yg_s[:, hp, :], yg_s[:, hp, :], z_b[:, h, :],
                                    ALU.mult), "c_v")
            g.w(vec, "c_v")
            vec.tensor_copy(y_b[:, h, :], yg_s[:, hp, :])
            g.inc(vec.tensor_tensor(ysq_b[:, hp, :], yg_s[:, hp, :], yg_s[:, hp, :],
                                    ALU.mult), "c_v")
            g.w(tn, "c_v")
            g.inc(tn.matmul(p_lg[0:1, 0:CHK], onesc_b[:], ysq_b[:, hp, :],
                            start=(h == 0), stop=(h == NH - 1)), "c_t")
            g.w(vec, "c_t")
            vec.tensor_scalar(ssd_S[:, h * HD:(h + 1) * HD], ssd_S[:, h * HD:(h + 1) * HD],
                              cdcol[:, h:h + 1], None, ALU.mult)
            g.inc(vec.engine_nop(), "c_v")
            g.w(vec, "c_v")
            g.inc(vec.tensor_tensor(ssd_S[:, h * HD:(h + 1) * HD],
                                    ssd_S[:, h * HD:(h + 1) * HD],
                                    p_st[:, (h % 8) * HD:(h % 8 + 1) * HD], ALU.add), "c_v")
        g.w(vec, "c_v")
        g.inc(vec.tensor_copy(ssd_Sb[:], ssd_S[:]), "c_v")
        g.w(vec, "c_t")
        g.inc(vec.tensor_scalar(rms_r[:], p_lg[0:1, 0:CHK], 1.0 / E, EPS,
                                ALU.mult, ALU.add), "c_v")
        g.w(sc, "c_v")
        sc.activation(rms_r[:], rms_r[:], AF.Ln)
        g.inc(sc.activation(rms_r[:], rms_r[:], AF.Exp, scale=-0.5), "c_s")
        g.w(tn, "c_s")
        g.inc(tn.matmul(p_sm[:, 0:CHK], ones_f[:], rms_r[:], start=True, stop=True), "c_t")
        g.w(vec, "c_t")
        g.inc(vec.tensor_copy(rbc[:], p_sm[:, 0:CHK]), "c_v")
        g.w(vec, "c_v")
        for h in range(NH):
            vec.tensor_tensor(y_b[:, h, :], y_b[:, h, :], rbc[:], ALU.mult)
        g.inc(vec.engine_nop(), "c_v")
        g.w(vec, "c_v")
        for h in range(NH):
            vec.tensor_scalar(y_b[:, h, :], y_b[:, h, :], nwc[:, h:h + 1], None, ALU.mult)
        g.inc(vec.engine_nop(), "c_v")
        ins = None
        for half in range(2):
            wob = woth
            g.w(gp, "c_t")
            ldma(gp, wob[:], woutT_b.ap()[:, half * 8:(half + 1) * 8, :], "s_gld")
            g.w(tn, "s_gld", "c_v")
            for tt_ in range(2):
                pmA = p_a if tt_ == 0 else p_c
                pmB = p_b if tt_ == 0 else p_d
                for ec in range(8):
                    eci = half * 8 + ec
                    tn.matmul(pmA[:, 0:512], y_b[:, eci, tt_ * 128:(tt_ + 1) * 128],
                              wob[:, ec, 0:512], start=(eci == 0), stop=(eci == 15))
                    ins = tn.matmul(pmB[:, 0:512], y_b[:, eci, tt_ * 128:(tt_ + 1) * 128],
                                    wob[:, ec, 512:1024], start=(eci == 0), stop=(eci == 15))
            g.inc(ins, "c_t")
        g.w(vec, "c_t", "s_st")
        for tt_ in range(2):
            pmA = p_a if tt_ == 0 else p_c
            pmB = p_b if tt_ == 0 else p_d
            vec.tensor_tensor(x2_tm[:, tt_, 0:512], pmA[:, 0:512], GA[:, 0:512], ALU.mult)
            vec.tensor_tensor(x2_tm[:, tt_, 512:1024], pmB[:, 0:512], GA[:, 512:1024],
                              ALU.mult)
        g.inc(vec.engine_nop(), "c_v")
        g.w(vec, "c_v")
        for tt_ in range(2):
            g.inc(vec.tensor_tensor(x2_tm[:, tt_, :], x2_tm[:, tt_, :],
                                    x_tm[:, tt_, :], ALU.add), "c_v")
        ln_block(x2_tm, h_f32, CB1, SBt)
        g.w(vec, "c_v")
        for a in range(2):
            g.inc(vec.tensor_copy(h2es[:, a, 0:DM], h_f32[:, a, :]), "c_v")
        for a in range(2):
            g.w(tn, "c_v")
            for dc in range(8):
                pt = p_c if dc % 2 == 0 else p_d
                g.wl(tn, "c_v", 1)
                g.inc(tn.transpose(pt[:, 0:128], h_f32[:, a, dc * 128:(dc + 1) * 128],
                                   ident_f[:]), "c_t")
                g.w(vec, "c_t")
                g.inc(vec.tensor_copy(h2fm[:, dc, :], pt[:, 0:128]), "c_v")
            g.w(tn, "c_v")
            mmacc(p_lg[:, a * 128:(a + 1) * 128],
                  [(wgt[:, dc, :], h2fm[:, dc, :]) for dc in range(8)])
            g.w(vec, "c_t", "s_gst")
            g.inc(vec.tensor_copy(logit_sb[:, a * 128:(a + 1) * 128],
                                  p_lg[:, a * 128:(a + 1) * 128]), "c_v")
        g.w(tn, "c_v")
        for a in range(2):
            pt = p_c if a == 0 else p_d
            g.inc(tn.transpose(pt[:, 0:NEXP],
                               logit_sb[0:NEXP, a * 128:(a + 1) * 128],
                               ident_f[0:NEXP, 0:NEXP]), "c_t")
            g.w(sc, "c_t", "s_st")
            g.inc(sc.activation(h2es[:, a, DM:DM + NEXP], pt[:, 0:NEXP], AF.Sigmoid), "c_s")
        g.w(sy, "c_v", "c_s")
        ldma(sy, ov[:, 2 * ci:2 * ci + 2, :], x2_tm[:], "s_st")
        ldma(sy, h2ev[:, 2 * ci:2 * ci + 2, :], h2es[:], "s_st")
        g.w(gp, "c_v")
        g.inc(gp.dma_start(cc_in.ap()[:, t0:t0 + CHK], logit_sb[:]), "s_gst", 16)

    mp.close()

    if upto == "x2":
        g.w(sy, "s_st")
        g.w(gp, "s_gst")
        nc.finalize()
        return nc

    fp = ExitStack()
    FM = lambda n, sh, dt: fp.enter_context(nc.sbuf_tensor(n, sh, dt))
    PF = lambda n, sh: fp.enter_context(nc.psum_tensor(n, sh, F32))
    Vv = FM("Vv", [128, TOT // 16], F32)
    m128x8 = FM("m128x8", [128, NEXP], F32)
    m8x128 = FM("m8x128", [NEXP, 128], F32)
    bst = FM("bst", [NEXP, 8], F32)
    nmid_sb = FM("nmid_sb", [128, 1], F32)
    cnt = FM("cnt", [128, 1], F32)
    vmxc = FM("vmxc", [128, 2], F32)
    mmrow = FM("mmrow", [1, 2, 128], F32)
    tvx = FM("tvx", [1, 2, NEXP], F32)
    throw = FM("throw", [1, NEXP], F32)
    onescol_f = FM("onescol_f", [128, 1], F32)
    thrb = FM("thrb", [16, NEXP], F32)
    iot2 = FM("iot2", [16, T // 16], F32)
    slot16 = FM("slot16", [16, nsel // 16], F32)
    cand = FM("cand", [16, T // 16], F32)
    mask = FM("mask", [16, T // 16], F32)
    candi = FM("candi", [16, NEXP, T // 16], I16)
    sgo = FM("sgo", [16, T // 16], I16)
    nf = FM("nf", [1, NEXP], U32)
    nff = FM("nff", [1, 1], F32)
    nfb = FM("nfb", [16, 1], F32)
    vmask = FM("vmask", [16, nsel // 16], F32)
    idxf = FM("idxf", [16, nsel // 16], F32)
    idxb = FM("idxb", [128, NEXP, nsel // 16], I16)
    gat_ch = FM("gat_ch", [128, 2, DE], BF16)
    gatF = FM("gatF", [128, 8, nsel], BF16)
    gcol = FM("gcol", [128, nsel // 128], F32)
    w1q = FM("w1q", [128, 2, 8, FF // 8], BF16)
    w2q = FM("w2q", [128, 2, 4, DM], BF16)
    b1c = FM("b1c", [128, FF // 128], F32)
    hidb = FM("hidb", [128, 2, 4, nsel], BF16)
    outacc = FM("outacc", [128, nsel // 128, DM], F32)
    b2g = FM("b2g", [128, DM], F32)
    b2row = FM("b2row", [1, DM], F32)
    gtmp8 = FM("gtmp8", [128, NEXP], BF16)
    p_h1 = PF("p_h1", [128, nsel])
    p_h2 = PF("p_h2", [128, nsel])
    p_o = PF("p_o", [128, 1024])

    g.w(gp, "s_gst", "s_st")
    g.w(vec, "s_st")
    g.w(sy, "s_st")
    g.inc(gp.collective_compute("AllGather", ALU.bypass, ins=[cc_in.ap()[:]],
                                outs=[cc_out.ap()[:]],
                                replica_groups=[list(range(ncores))]), "c_cc")
    gp.iota(iot2[:], [[16, T // 16]], base=0, channel_multiplier=1,
            allow_small_or_imprecise_dtypes=True)
    g.inc(gp.iota(slot16[:], [[16, nsel // 16]], base=0, channel_multiplier=1,
                  allow_small_or_imprecise_dtypes=True), "c_g")
    g.inc(vec.memset(idxb[:], 0), "c_v")

    FV = TOT // 16
    vec.memset(m128x8[:], 1.0)
    g.inc(vec.memset(m8x128[:], 1.0), "c_v")
    g.w(gp, "c_v")
    gp.affine_select(m128x8[:], m128x8[:], [[-16, NEXP]], ALU.is_ge, 0.0,
                     base=0, channel_multiplier=1)
    gp.affine_select(m128x8[:], m128x8[:], [[16, NEXP]], ALU.is_ge, 0.0,
                     base=15, channel_multiplier=-1)
    gp.affine_select(m8x128[:], m8x128[:], [[1, 128]], ALU.is_ge, 0.0,
                     base=0, channel_multiplier=-16)
    g.inc(gp.affine_select(m8x128[:], m8x128[:], [[-1, 128]], ALU.is_ge, 0.0,
                           base=15, channel_multiplier=16), "c_g")
    g.w(sy, "c_cc")
    PPC = 16 // ncores
    for e in range(NEXP):
        for cc_ in range(ncores):
            ldma(sy, Vv[16 * e + cc_ * PPC:16 * e + (cc_ + 1) * PPC, :],
                 cc_out.ap()[cc_, e, :].rearrange("(p f) -> p f", p=PPC), "s_ld")
    fmx = outacc[:, 0, :]
    fmn = outacc[:, 1, :]
    g.w(vec, "s_ld")
    w2_ = FV // 2
    vec.tensor_tensor(fmx[:, 0:w2_], Vv[:, 0:w2_], Vv[:, w2_:FV], ALU.max)
    g.inc(vec.tensor_tensor(fmn[:, 0:w2_], Vv[:, 0:w2_], Vv[:, w2_:FV], ALU.min),
          "c_v")
    w2_ //= 2
    while w2_ >= 1:
        g.w(vec, "c_v")
        vec.tensor_tensor(fmx[:, 0:w2_], fmx[:, 0:w2_], fmx[:, w2_:2 * w2_], ALU.max)
        g.inc(vec.tensor_tensor(fmn[:, 0:w2_], fmn[:, 0:w2_], fmn[:, w2_:2 * w2_],
                                ALU.min), "c_v")
        w2_ //= 2
    g.w(vec, "c_v")
    vec.tensor_copy(vmxc[:, 0:1], fmx[:, 0:1])
    g.inc(vec.tensor_copy(vmxc[:, 1:2], fmn[:, 0:1]), "c_v")
    g.w(tn, "c_v")
    tn.transpose(p_h2[0:1, 0:128], vmxc[:, 0:1], ident_f[:])
    g.inc(tn.transpose(p_h2[0:1, 128:256], vmxc[:, 1:2], ident_f[:]), "c_t")
    g.w(vec, "c_t")
    g.inc(vec.tensor_copy(mmrow[:], p_h2[0:1, 0:256]), "c_v")
    mmv = mmrow[:, :, :].rearrange("o i (a b) -> o i a b", a=NEXP)
    b2_ = 8
    while b2_ >= 1:
        g.w(vec, "c_v")
        vec.tensor_tensor(mmv[:, 0, :, 0:b2_], mmv[:, 0, :, 0:b2_],
                          mmv[:, 0, :, b2_:2 * b2_], ALU.max)
        g.inc(vec.tensor_tensor(mmv[:, 1, :, 0:b2_], mmv[:, 1, :, 0:b2_],
                                mmv[:, 1, :, b2_:2 * b2_], ALU.min), "c_v")
        b2_ //= 2
    g.w(vec, "c_v")
    g.inc(vec.tensor_copy(tvx[:, :, :].rearrange("o i a -> o i a ()"),
                          mmv[:, :, :, 0:1]), "c_v")
    g.w(tn, "c_v")
    tn.transpose(p_h2[0:NEXP, 0:1], tvx[0:1, 0, :], ident_f[0:1, 0:1])
    g.inc(tn.transpose(p_h2[0:NEXP, 1:2], tvx[0:1, 1, :], ident_f[0:1, 0:1]), "c_t")
    g.w(vec, "c_t")
    vec.tensor_copy(bst[:, 1:2], p_h2[0:NEXP, 0:1])
    g.inc(vec.tensor_copy(bst[:, 6:7], p_h2[0:NEXP, 1:2]), "c_v")
    g.w(sc, "c_v")
    g.inc(sc.activation(bst[:, 7:8], bst[:, 6:7], AF.Abs), "c_s")
    g.w(vec, "c_s")
    g.inc(vec.tensor_scalar(bst[:, 7:8], bst[:, 7:8], 2.0 ** -10, 1.0,
                            ALU.mult, ALU.add), "c_v")
    g.w(vec, "c_v")
    g.inc(vec.tensor_tensor(bst[:, 0:1], bst[:, 6:7], bst[:, 7:8], ALU.subtract),
          "c_v")
    for it in range(32):
        g.w(vec, "c_v", "c_s")
        g.inc(vec.tensor_tensor(bst[:, 2:3], bst[:, 0:1], bst[:, 1:2], ALU.add), "c_v")
        g.w(vec, "c_v")
        g.inc(vec.tensor_scalar(bst[:, 2:3], bst[:, 2:3], 0.5, None, ALU.mult), "c_v")
        g.w(vec, "c_v")
        vec.tensor_scalar(bst[:, 3:4], bst[:, 2:3], -1.0, None, ALU.mult)
        vec.tensor_tensor(bst[:, 5:6], bst[:, 2:3], bst[:, 0:1], ALU.subtract)
        g.inc(vec.tensor_tensor(bst[:, 6:7], bst[:, 1:2], bst[:, 2:3], ALU.subtract),
              "c_v")
        g.w(tn, "c_v")
        g.inc(tn.matmul(p_h1[:, 0:1], m8x128[:], bst[:, 3:4], start=True, stop=True),
              "c_t")
        g.w(vec, "c_t")
        g.inc(vec.tensor_copy(nmid_sb[:], p_h1[:, 0:1]), "c_v")
        g.w(sc, "c_v")
        g.inc(sc.activation(outacc.bitcast(BF16)[:, 0, 0:FV], Vv[:], AF.Sign,
                            bias=nmid_sb[:, 0:1], accum_out=cnt[:, 0:1]), "c_s")
        g.w(tn, "c_s")
        g.inc(tn.matmul(p_h2[0:NEXP, 0:1], m128x8[:], cnt[:], start=True, stop=True),
              "c_t")
        g.w(vec, "c_t")
        g.inc(vec.tensor_scalar(bst[:, 4:5], p_h2[0:NEXP, 0:1], -(TOT / 2.0), None,
                                ALU.is_ge), "c_v")
        g.w(vec, "c_v")
        vec.tensor_tensor(bst[:, 5:6], bst[:, 5:6], bst[:, 4:5], ALU.mult)
        g.inc(vec.tensor_tensor(bst[:, 6:7], bst[:, 6:7], bst[:, 4:5], ALU.mult),
              "c_v")
        g.w(vec, "c_v")
        vec.tensor_tensor(bst[:, 0:1], bst[:, 0:1], bst[:, 5:6], ALU.add)
        g.inc(vec.tensor_tensor(bst[:, 1:2], bst[:, 2:3], bst[:, 6:7], ALU.add),
              "c_v")
    g.w(tn, "c_v")
    g.inc(tn.transpose(p_h2[0:1, 0:NEXP], bst[:, 1:2], ident_f[0:NEXP, 0:NEXP]),
          "c_t")
    g.w(vec, "c_t")
    g.inc(vec.tensor_copy(throw[:], p_h2[0:1, 0:NEXP]), "c_v")
    if upto == "thr":
        g.w(gp, "s_gst")
        g.w(sy, "s_st")
        g.w(vec, "c_v")
        nc.finalize()
        return nc
    g.w(gp, "c_v", "c_g")
    ins = None
    for e in range(NEXP):
        ins = gp.partition_broadcast(thrb[:, e:e + 1], throw[0:1, e:e + 1], 16)
    g.inc(ins, "c_g")
    for e in range(NEXP):
        g.w(sy, "c_v", "c_g")
        ldma(sy, cand[:], cc_in.ap()[e:e + 1, :].rearrange("o (f p) -> (o p) f", p=16),
             "s_ld", allow_slow_non_contiguous=True)
        g.w(vec, "s_ld")
        g.inc(vec.tensor_scalar(mask[:], cand[:], thrb[:, e:e + 1], None, ALU.is_ge), "c_v")
        g.w(vec, "c_v")
        g.inc(vec.scalar_tensor_tensor(cand[:], iot2[:], 1.0, mask[:], ALU.add,
                                       ALU.mult), "c_v")
        g.w(vec, "c_v")
        g.inc(vec.tensor_scalar(cand[:], cand[:], 1.0, None, ALU.subtract), "c_v")
        g.w(vec, "c_v")
        g.inc(vec.tensor_copy(candi[:, e, :], cand[:]), "c_v")
    for e in range(NEXP):
        g.w(gp, "c_v")
        g.inc(gp.sparse_gather(sgo[:], candi[:, e, :], num_found=nf[0:1, e:e + 1]), "c_g")
        g.w(vec, "c_g")
        g.inc(vec.tensor_copy(nff[:], nf[0:1, e:e + 1]), "c_v")
        g.w(gp, "c_v")
        g.inc(gp.partition_broadcast(nfb[:], nff[:], 16), "c_g")
        g.w(vec, "c_g")
        vec.tensor_scalar(vmask[:], slot16[:], nfb[:], None, ALU.is_lt)
        g.inc(vec.tensor_copy(idxf[:], sgo[:, 0:nsel // 16]), "c_v")
        g.w(vec, "c_v")
        g.inc(vec.tensor_tensor(idxf[:], idxf[:], vmask[:], ALU.mult), "c_v")
        g.w(vec, "c_v")
        g.inc(vec.tensor_scalar(vmask[:], vmask[:], 1.0, None, ALU.subtract), "c_v")
        g.w(vec, "c_v")
        g.inc(vec.tensor_scalar(vmask[:], vmask[:], -float(DUMMY), None, ALU.mult), "c_v")
        g.w(vec, "c_v")
        g.inc(vec.tensor_tensor(idxf[:], idxf[:], vmask[:], ALU.add), "c_v")
        g.w(vec, "c_v")
        g.inc(vec.tensor_copy(idxb[0:16, e, :], idxf[:]), "c_v")
        g.w(gp, "c_v")
        g.inc(gp.dma_start(idxb[16:32, e, :], idxb[0:16, e, :]), "s_gst", 16)

    if upto == "sg":
        g.w(gp, "s_gst")
        g.w(sy, "s_st")
        g.w(vec, "c_v")
        nc.finalize()
        return nc

    w1v = w1.rearrange("e (c p) f -> e p c f", p=128)
    w2v = w2.rearrange("e (c p) d -> e p c d", p=128)
    for e in range(NEXP):
        g.w(gp, "s_gst", "c_t", "c_s", "s_gat")
        for cq in range(nsel // 128):
            bufq = cq % 2
            g.w(gp, "c_t")
            g.inc(gp.dma_gather(gat_ch[:, bufq:bufq + 1, :], h2e.ap()[:],
                                idxb[:, e, 8 * cq:8 * cq + 8], 128, 128, DE),
                  "s_gat", 16)
            g.w(vec, "s_gat")
            g.inc(vec.tensor_copy(gcol[:, cq:cq + 1],
                                  gat_ch[:, bufq, DM + e:DM + e + 1]), "c_v")
            g.w(tn, "s_gat")
            four_slot = 2 * nsel >= 2048
            for dc in range(8):
                sl = 1024 * ((dc % 4) // 2) if four_slot else 0
                pt = (p_h1 if dc % 2 == 0 else p_h2).bitcast(BF16)
                g.wl(tn, "c_v", 3 if four_slot else 1)
                g.inc(tn.transpose(pt[:, sl:sl + 128],
                                   gat_ch[:, bufq, dc * 128:(dc + 1) * 128],
                                   ident_b[:]), "c_t")
                g.w(vec, "c_t")
                g.inc(vec.tensor_copy(gatF[:, dc, cq * 128:(cq + 1) * 128],
                                      pt[:, sl:sl + 128]), "c_v")
        g.w(sy, "c_v")
        ldma(sy, b1c[:], b1[e].rearrange("o (t p) -> (o p) t", p=128), "s_ld", allow_slow_non_contiguous=True)
        ldma(sy, b2row[:], b2[e], "s_ld")
        g.w(tn, "c_v", "s_ld")
        for hh in range(2):
            g.inc(tn.matmul(p_o[:, hh * 512:(hh + 1) * 512], ones_f[:],
                            b2row[0:1, hh * 512:(hh + 1) * 512], start=True, stop=True), "c_t")
        g.w(vec, "c_t")
        g.inc(vec.tensor_copy(b2g[:], p_o[:, 0:1024]), "c_v")
        g.w(vec, "c_v")
        g.inc(vec.tensor_tensor(b2g[:], b2g[:], GB[:], ALU.mult), "c_v")
        LAG_T = 4 + nsel // 128
        LAG0_T = LAG_T + 8 * (nsel // 128) + 2
        for q in range(8):
            qb = q % 2
            ff0 = q * (FF // 8)
            lag = LAG0_T if q < 2 else LAG_T
            if e == 0 and q < 2:
                g.w(gp, "c_t", "c_s")
                g.w(sy, "c_t", "c_s")
            else:
                g.wl(gp, "c_t", lag)
                g.wl(gp, "c_s", 4)
                g.wl(sy, "c_t", lag)
                g.wl(sy, "c_s", 4)
            ldma(gp, w1q[:, qb], w1v[e][:, :, ff0:ff0 + FF // 8], "s_gld")
            ldma(gp, w2q[:, qb], w2v[e][:, 4 * q:4 * q + 4, :], "s_gld")
            g.w(tn, "s_gld", "s_ld", "s_gat", "c_v")
            for ffc in range(4):
                ph = p_h1 if ffc % 2 == 0 else p_h2
                g.wl(tn, "c_s", 1)
                ins = None
                for s0 in range(0, nsel, 512):
                    sw = min(512, nsel - s0)
                    for dc in range(8):
                        ins = tn.matmul(ph[:, s0:s0 + sw],
                                        w1q[:, qb, dc, ffc * 128:(ffc + 1) * 128],
                                        gatF[:, dc, s0:s0 + sw],
                                        start=(dc == 0), stop=(dc == 7))
                g.inc(ins, "c_t")
                g.w(sc, "c_t")
                g.inc(sc.activation(hidb[:, qb, ffc, :], ph[:, 0:nsel], AF.Gelu,
                                    bias=b1c[:, 4 * q + ffc:4 * q + ffc + 1]), "c_s")
            g.w(tn, "c_s")
            for sct in range(nsel // 128):
                g.w(tn, "c_v")
                ins = None
                for ffc in range(4):
                    for hh in range(2):
                        ins = tn.matmul(p_o[:, hh * 512:(hh + 1) * 512],
                                        hidb[:, qb, ffc, sct * 128:(sct + 1) * 128],
                                        w2q[:, qb, ffc, hh * 512:(hh + 1) * 512],
                                        start=(ffc == 0), stop=(ffc == 3))
                g.inc(ins, "c_t")
                g.w(vec, "c_t")
                if q == 0:
                    g.inc(vec.tensor_copy(outacc[:, sct, :], p_o[:, 0:1024]), "c_v")
                else:
                    g.inc(vec.tensor_tensor(outacc[:, sct, :], outacc[:, sct, :],
                                            p_o[:, 0:1024], ALU.add), "c_v")
        g.w(vec, "c_v")
        for sct in range(nsel // 128):
            vec.tensor_tensor(outacc[:, sct, :], outacc[:, sct, :], GB[:], ALU.mult)
        g.inc(vec.engine_nop(), "c_v")
        g.w(vec, "c_v")
        for sct in range(nsel // 128):
            vec.tensor_tensor(outacc[:, sct, :], outacc[:, sct, :], b2g[:], ALU.add)
        g.inc(vec.engine_nop(), "c_v")
        g.w(vec, "c_v")
        for sct in range(nsel // 128):
            vec.tensor_scalar(outacc[:, sct, :], outacc[:, sct, :], gcol[:, sct:sct + 1],
                              None, ALU.mult)
        g.inc(vec.engine_nop(), "c_v")
        g.w(gp, "c_v", "s_gat", "s_st")
        if upto == "noscat":
            g.inc(gp.dma_start(scat_dbg.ap()[:], outacc[:]), "s_gat", 16)
        else:
            for cq in range(nsel // 128):
                g.inc(gp.dma_scatter_add(out.ap()[:], outacc[:, cq:cq + 1, :],
                                         idxb[:, e, 8 * cq:8 * cq + 8], 128,
                                         128, DM), "s_gat", 16)

    g.w(gp, "s_gat")
    g.w(sy, "s_st")
    nc.finalize()
    return nc


_CACHE = {}


def _get_nc(T, ncores, nsel, upto):
    key = (T, ncores, nsel, upto)
    if key not in _CACHE:
        _CACHE[key] = build(T, ncores, nsel, upto)
    return _CACHE[key]


def kernel(L0_attn_u, L0_attn_in_proj_w, L0_attn_conv_w, L0_attn_conv_b,
           L0_attn_dt_bias, L0_attn_A_log, L0_attn_Dp, L0_attn_norm_w,
           L0_attn_out_proj_w, L0_ffn_Wg, L0_ffn_W1, L0_ffn_b1, L0_ffn_W2,
           L0_ffn_b2, L0_c, L0_wmod):
    B, T, _ = L0_attn_u.shape
    ncores = B
    nsel = 1280 if T * ncores >= 32768 else max(128, ((T * ncores // 4) // ncores * 2 + 127) // 128 * 128)
    nc = _get_nc(T, ncores, nsel, globals().get("STAGE", "full"))
    A = np.ascontiguousarray
    f32 = np.float32
    shared = {
        "w_inT": A(np.asarray(L0_attn_in_proj_w, f32).T),
        "conv_w": A(np.asarray(L0_attn_conv_w, f32)),
        "conv_b": A(np.asarray(L0_attn_conv_b, f32).reshape(1, CONVD)),
        "dt_b_c": A(np.asarray(L0_attn_dt_bias, f32).reshape(NH, 1)),
        "a_log_c": A(np.asarray(L0_attn_A_log, f32).reshape(NH, 1)),
        "d_p_c": A(np.asarray(L0_attn_Dp, f32).reshape(NH, 1)),
        "norm_w": A(np.asarray(L0_attn_norm_w, f32).reshape(1, E)),
        "w_outT": A(np.asarray(L0_attn_out_proj_w, f32).T),
        "wg": A(np.asarray(L0_ffn_Wg, f32)),
        "w1": A(np.asarray(L0_ffn_W1, f32)),
        "b1": A(np.asarray(L0_ffn_b1, f32).reshape(NEXP, 1, FF)),
        "w2": A(np.asarray(L0_ffn_W2, f32)),
        "b2": A(np.asarray(L0_ffn_b2, f32).reshape(NEXP, 1, DM)),
        "wmod": A(np.asarray(L0_wmod, f32)),
    }
    uu = np.asarray(L0_attn_u, f32)
    cc = np.asarray(L0_c, f32)
    in_maps = []
    for b in range(ncores):
        m = dict(shared)
        m["u"] = A(uu[b])
        m["c_row"] = A(cc[b].reshape(1, DM))
        in_maps.append(m)
    res = run_bass_kernel_spmd(nc, in_maps, core_ids=list(range(ncores)))
    return np.stack([np.asarray(r["out"])[0:T] for r in res.results])
